# version 31
# baseline (speedup 1.0000x reference)
import numpy as np
import ml_dtypes
import concourse.bass as bass
import concourse.mybir as mybir
from concourse.bass_utils import run_bass_kernel_spmd

F32 = mybir.dt.float32
BF16 = mybir.dt.bfloat16
I32 = mybir.dt.int32
AF = mybir.ActivationFunctionType
ALU = mybir.AluOpType
AX = mybir.AxisListType
BFNP = ml_dtypes.bfloat16

D = 1024
DC = 8
T = 2048
TT = 256
NT = T // TT
NSEQ = 2
RSQRT_C = 6.144147586977132e18
LN_EPS = 1e-5
DEN_EPS = 1e-6
INV_D = 1.0 / D


class Ch:
    def __init__(self, nc, name):
        self.sem = nc.alloc_semaphore(name)
        self.n = 0


class Builder:

    ENGINES = ("sync", "gpsimd", "vector", "scalar", "tensor")

    def __init__(self, nc):
        self.nc = nc
        self.ops = {e: [] for e in self.ENGINES}
        self.CV = Ch(nc, "cv")
        self.CS = Ch(nc, "cs")
        self.CT = Ch(nc, "ct")
        self.CG = Ch(nc, "cg")
        self.SC = Ch(nc, "sc")
        self.SX = Ch(nc, "sx")
        self.SO = Ch(nc, "so")
        self.SW = [Ch(nc, "sw0"), Ch(nc, "sw1"),
                   Ch(nc, "sw2"), Ch(nc, "sw3")]
        self.CG2 = Ch(nc, "cg2")

    def ch(self, eng):
        return {"vector": self.CV, "scalar": self.CS, "tensor": self.CT,
                "gpsimd": self.CG}[eng]

    def wait(self, eng, ch, val):
        if val > 0:
            self.ops[eng].append(lambda e, s=ch.sem, v=val: e.wait_ge(s, v))

    def barrier(self, eng, *chs):
        for c in chs:
            self.wait(eng, c, c.n)

    def op(self, eng, builder, ev=True):
        ch = self.ch(eng)
        if ev:
            ch.n += 1
            def fn(e, b=builder, sem=ch.sem):
                b(e).then_inc(sem, 1)
            self.ops[eng].append(fn)
            return ch.n
        self.ops[eng].append(lambda e, b=builder: b(e))
        return None

    def dma(self, eng, ch, out, in_, serial=False, **kw):
        ch.n += 16
        def fn(e, o=out, i=in_, k=dict(kw), sem=ch.sem):
            e.dma_start(out=o, in_=i, **k).then_inc(sem, 16)
        self.ops[eng].append(fn)
        if serial:
            self.wait(eng, ch, ch.n)
        return ch.n

    def v(self, builder, sync=True):
        ev = self.op("vector", builder)
        if sync:
            self.wait("vector", self.CV, ev)
        return ev

    def s(self, builder, sync=True):
        ev = self.op("scalar", builder)
        if sync:
            self.wait("scalar", self.CS, ev)
        return ev

    def t(self, builder, ev=False):
        return self.op("tensor", builder, ev=ev)


def build_program(debug=False):
    nc = bass.Bass("TRN2", target_bir_lowering=False, debug=False)

    X = nc.dram_tensor("x", [NSEQ, T, D], F32, kind="ExternalInput").ap()
    Y = nc.dram_tensor("y", [NSEQ, T, D], F32, kind="ExternalOutput").ap()
    QKVW = nc.dram_tensor("qkv_w", [D, 3 * D], F32, kind="ExternalInput").ap()
    QKVB = nc.dram_tensor("qkv_b", [3 * D], F32, kind="ExternalInput").ap()
    GATEW = nc.dram_tensor("gate_w", [D, D], F32, kind="ExternalInput").ap()
    GATEB = nc.dram_tensor("gate_b", [D], F32, kind="ExternalInput").ap()
    LNG = nc.dram_tensor("ln_g", [D], F32, kind="ExternalInput").ap()
    LNB = nc.dram_tensor("ln_b", [D], F32, kind="ExternalInput").ap()
    WR = nc.dram_tensor("W_r", [D, D], F32, kind="ExternalInput").ap()
    WK = nc.dram_tensor("W_k", [D, D], F32, kind="ExternalInput").ap()
    WV = nc.dram_tensor("W_v", [D, D], F32, kind="ExternalInput").ap()
    MUR = nc.dram_tensor("mu_r", [D], F32, kind="ExternalInput").ap()
    MUK = nc.dram_tensor("mu_k", [D], F32, kind="ExternalInput").ap()
    IDF = nc.dram_tensor("ident_f", [128, 128], F32, kind="ExternalInput").ap()
    IDB = nc.dram_tensor("ident_b", [128, 128], BF16, kind="ExternalInput").ap()
    CM = nc.dram_tensor("causal_m", [128, 128], BF16, kind="ExternalInput").ap()

    DBG1 = DBG2 = None
    if debug:
        DBG1 = nc.dram_tensor("dbg_y1", [NSEQ, T, D], F32, kind="ExternalOutput").ap()
        DBG2 = nc.dram_tensor("dbg_y2", [NSEQ, T, D], F32, kind="ExternalOutput").ap()
        DQT = nc.dram_tensor("dbg_qT", [128, DC, TT], BF16, kind="ExternalOutput").ap()
        DKT = nc.dram_tensor("dbg_kT", [128, DC, TT], BF16, kind="ExternalOutput").ap()
        DXN = nc.dram_tensor("dbg_xn2T", [128, DC, TT], BF16, kind="ExternalOutput").ap()
        DVB = nc.dram_tensor("dbg_vb", [128, 2, D], BF16, kind="ExternalOutput").ap()
        DAT = nc.dram_tensor("dbg_aT", [128, 2, TT], BF16, kind="ExternalOutput").ap()
        DST = nc.dram_tensor("dbg_stats", [128, 64], F32, kind="ExternalOutput").ap()
    QS = nc.dram_tensor("qstream", [8, 128, DC, 4, 128], BF16, kind="Internal")
    BROW = nc.dram_tensor("brow", [1, 4 * D], F32, kind="Internal")

    A = nc.alloc_sbuf_tensor
    wT = A("wT", [128, DC, 3 * D], BF16)
    wstr = [A(f"wstr{i}", [128, DC, 512], BF16) for i in range(2)]
    KT = A("KT", [128, DC, T], BF16)
    Vb = A("Vb", [128, T // 128, D], BF16)
    xb = A("xb", [128, 2, D], F32)
    y1 = A("y1", [128, 2, D], F32)
    y2 = A("y2", [128, 2, D], F32)
    xst = A("xst", [128, 2, D], F32)
    xnT = A("xnT", [128, DC, TT + 1], BF16)
    xrT = A("xrT", [128, DC, TT], BF16)
    xkT = A("xkT", [128, DC, TT], BF16)
    kaT = xrT
    qT = xkT
    xn2T = A("xn2T", [128, DC, TT], BF16)
    aT = A("aT", [128, 2, TT], BF16)
    S1 = A("S1", [128, 2048], F32)
    S2 = A("S2", [128, 2048], F32)
    brep = A("brep", [128, 4 * D], BF16)
    murk = A("murk", [128, DC, 2], F32)
    gcol = A("gcol", [128, DC], F32)
    lnbcol = A("lnbcol", [128, DC], F32)
    lnbb = A("lnbb", [128, DC], BF16)
    idf = A("idf", [128, 128], F32)
    idb = A("idb", [128, 128], BF16)
    cmask = A("cmask", [128, 128], BF16)
    onesb = A("onesb", [128, 1], BF16)
    brqb = A("brqb", [1, D], BF16)
    onerow = A("onerow", [1, 128], BF16)
    wstage = A("wstage", [128, 1024], BF16)
    stats = A("stats", [128, 64], F32)
    ibits = A("ibits", [128, 8], F32)
    stag8 = A("stag8", [8, 128], F32)
    carry = [A(f"carry{i}", [128, DC], BF16) for i in range(2)]

    def scol(i, w=2):
        return stats[:, i:i + w]
    C_SUM, C_SQ, C_NMU, C_VAR, C_V1, C_RST = (
        scol(0), scol(2), scol(4), scol(6), scol(8), scol(10))
    C_T1, C_V2, C_S2, C_RD, C_N1 = scol(12), scol(14), scol(16), scol(18), scol(20)

    PA = nc.alloc_psum_tensor("PA", [128, 1024], F32)
    PB = nc.alloc_psum_tensor("PB", [128, 1024], F32)
    PC = nc.alloc_psum_tensor("PC", [128, 1024], F32)
    PT = nc.alloc_psum_tensor("PT", [128, 2048], BF16)
    PT32 = PT[:].bitcast(F32)

    B = Builder(nc)
    CV, CS, CT, CG = B.CV, B.CS, B.CT, B.CG

    B.dma("sync", B.SC, idf[:], IDF, serial=True)
    B.dma("sync", B.SC, idb[:], IDB, serial=True)
    e_cst = B.dma("sync", B.SC, cmask[:], CM, serial=True)
    B.wait("vector", B.SC, e_cst)
    B.v(lambda v: v.memset(onesb[:], 1.0), sync=False)
    B.v(lambda v: v.memset(onerow[:], 1.0), sync=False)

    def redist(src_1d, dst_col):
        B.barrier("sync", CS)
        e_ld = B.dma("sync", B.SC, stag8[:],
                     src_1d.rearrange("(c p) -> c p", p=128), serial=True)
        B.wait("tensor", B.SC, e_ld)
        B.barrier("tensor", CS, CV)
        e_tp = B.t(lambda t: t.transpose(out=PC[:, 0:8], in_=stag8[:],
                                         identity=idf[0:8, 0:8]), ev=True)
        B.wait("scalar", CT, e_tp)
        return B.s(lambda s_, d=dst_col: s_.copy(out=d, in_=PC[:, 0:8]),
                   sync=False)

    redist(MUR, murk[:, :, 0])
    redist(MUK, murk[:, :, 1])
    redist(LNG, gcol[:])
    e_lnb = redist(LNB, lnbcol[:])
    B.wait("vector", CS, e_lnb)
    B.v(lambda v: v.tensor_copy(out=lnbb[:], in_=lnbcol[:]), sync=False)

    witems = []
    for (WSRC, off, scale) in [(WR, 0, 1.0), (WK, D, 1.0), (WV, 2 * D, 0.5)]:
        for dp in range(4):
            witems.append((WSRC, off, scale, dp))
    wld = {}
    cast_done = {}

    def w_issue(it):
        WSRC, off, scale, dp = witems[it]
        par = it % 2
        sbuf = S1 if par == 0 else S2
        if it - 2 in cast_done:
            B.wait("sync", CV, cast_done[it - 2])
        wld[it] = B.dma("sync", B.SW[par],
                        sbuf[:].rearrange("p (a b) -> p a b", a=2),
                        WSRC[dp * 256:(dp + 1) * 256, :]
                        .rearrange("(a p) d -> p a d", p=128))

    w_issue(0)
    w_issue(1)
    wstage2 = xrT[:].rearrange("p c j -> p (c j)")
    tp_done = {}
    ev_done = {}
    for it, (WSRC, off, scale, dp) in enumerate(witems):
        par = it % 2
        sbuf = S1 if par == 0 else S2
        for half in range(2):
            k = it * 2 + half
            dd = dp * 2 + half
            stg = wstage[:] if k % 2 == 0 else wstage2[:, 0:1024]
            pth = (k % 2) * 1024
            if k - 2 in tp_done:
                B.wait("vector", CT, tp_done[k - 2])
            B.wait("vector", B.SW[par], wld[it])
            e_c = B.v(lambda v, sc=scale, sbuf=sbuf, half=half, stg=stg:
                      v.tensor_scalar(
                          out=stg, in0=sbuf[:, half * 1024:(half + 1) * 1024],
                          scalar1=sc, scalar2=None, op0=ALU.mult), sync=False)
            cast_done[it] = e_c
            B.wait("tensor", CV, e_c)
            if k - 2 in ev_done:
                B.wait("tensor", CS, ev_done[k - 2])
            for cc in range(8):
                e_tp = B.t(lambda t, cc=cc, stg=stg, pth=pth: t.transpose(
                    out=PT[:, pth + cc * 128:pth + (cc + 1) * 128],
                    in_=stg[:, cc * 128:(cc + 1) * 128], identity=idb[:]),
                    ev=(cc == 7))
            tp_done[k] = e_tp
            B.wait("scalar", CT, e_tp)
            ev_done[k] = B.s(lambda s_, dd=dd, off=off, pth=pth: s_.copy(
                out=wT[:, :, off + dd * 128: off + (dd + 1) * 128],
                in_=PT[:, pth:pth + 1024].rearrange("p (c j) -> p c j", c=8)),
                sync=False)
        if it + 2 < len(witems):
            w_issue(it + 2)

    def slab_of(part, jg):
        if part == 0:
            if jg < 2:
                return jg * 4
            if jg < 4:
                return 16 + (jg - 2) * 4
            return 24 + (jg - 4) * 4
        return 8 + jg * 4

    fold_regions = [(PA, 0), (PA, 512), (PB, 0), (PB, 512),
                    (PC, 0), (PC, 512), (PT32, 0), (PT32, 512)]
    B.barrier("tensor", CS, CV)
    fitems = []
    for part, (WSRC, jtot) in enumerate([(QKVW, 3 * D), (GATEW, D)]):
        for jg in range(jtot // 512):
            fitems.append((part, WSRC, jg))
    Vbig = Vb[:].rearrange("p a b -> p (a b)")
    y1f = y1[:].rearrange("p s d -> p (s d)")
    xstf0 = xst[:].rearrange("p s d -> p (s d)")
    f_ld = {}
    f_cast = {}
    f_mm = {}

    def fold_issue(it):
        part, WSRC, jg = fitems[it]
        par = it % 2
        d0, d1 = (S1[:], S2[:]) if par == 0 else (xstf0, y1f)
        ch0, ch1 = (B.SW[0], B.SW[1]) if par == 0 else (B.SW[2], B.SW[3])
        if it - 2 in f_cast:
            B.wait("sync", CV, f_cast[it - 2])
            B.wait("sync", CT, f_mm[it - 2])
        if par == 0:
            v0 = S1[:].rearrange("p (a b) -> p a b", a=4)
            v1 = S2[:].rearrange("p (a b) -> p a b", a=4)
        else:
            v0 = xst[:].rearrange("p s (a b) -> p (s a) b", a=2)
            v1 = y1[:].rearrange("p s (a b) -> p (s a) b", a=2)
        e0 = B.dma("sync", ch0, v0,
                   WSRC[0:512, jg * 512:(jg + 1) * 512]
                   .rearrange("(a p) d -> p a d", p=128))
        e1 = B.dma("sync", ch1, v1,
                   WSRC[512:1024, jg * 512:(jg + 1) * 512]
                   .rearrange("(a p) d -> p a d", p=128))
        f_ld[it] = (ch0, e0, ch1, e1)

    B.barrier("sync", CV, CT)
    fold_issue(0)
    fold_issue(1)
    for it, (part, WSRC, jg) in enumerate(fitems):
        grp = slab_of(part, jg) // 4
        ps, pcol = fold_regions[grp]
        par = it % 2
        d0, d1 = (S1[:], S2[:]) if par == 0 else (xstf0, y1f)
        ch0, e0, ch1, e1 = f_ld[it]
        B.wait("tensor", ch0, e0)
        B.wait("tensor", ch1, e1)
        for cc in range(8):
            sbuf = d0 if cc < 4 else d1
            e_bm = B.t(lambda t, cc=cc, ps=ps, pcol=pcol, sbuf=sbuf: t.matmul(
                out=ps[0:1, pcol:pcol + 512], lhsT=lnbcol[:, cc:cc + 1],
                rhs=sbuf[:, (cc % 4) * 512:(cc % 4 + 1) * 512],
                start=(cc == 0), stop=(cc == 7)), ev=(cc == 7))
        f_mm[it] = e_bm
        B.wait("vector", ch0, e0)
        B.wait("vector", ch1, e1)
        vcol = (it % 2) * 4096
        if it - 2 in f_cast:
            pass
        B.wait("vector", B.CG if it % 2 == 0 else B.CG2,
               (B.CG if it % 2 == 0 else B.CG2).n)
        for cc in range(8):
            sbuf = d0 if cc < 4 else d1
            e_c = B.v(lambda v, cc=cc, sbuf=sbuf, vcol=vcol: v.tensor_scalar(
                out=Vbig[:, vcol + cc * 512:vcol + (cc + 1) * 512],
                in0=sbuf[:, (cc % 4) * 512:(cc % 4 + 1) * 512],
                scalar1=gcol[:, cc:cc + 1], scalar2=None, op0=ALU.mult),
                sync=False)
        f_cast[it] = e_c
        gch = B.CG if it % 2 == 0 else B.CG2
        B.wait("gpsimd", CV, e_c)
        B.dma("gpsimd", gch, QS.ap()[grp, :, :, :, :],
              Vbig[:, vcol:vcol + 4096].rearrange("p (c a b) -> p c a b",
                                                  c=8, a=4))
        if it + 2 < len(fitems):
            fold_issue(it + 2)
    B.wait("sync", B.CG2, B.CG2.n)

    B.barrier("vector", CT)
    for grp in range(8):
        ps, pcol = fold_regions[grp]
        B.barrier("vector", CG)
        e_ev = B.v(lambda v, ps=ps, pcol=pcol: v.tensor_copy(
            out=S2[0:1, 0:512], in_=ps[0:1, pcol:pcol + 512]), sync=False)
        B.wait("gpsimd", CV, e_ev)
        B.dma("gpsimd", CG, BROW.ap()[0:1, grp * 512:(grp + 1) * 512],
              S2[0:1, 0:512], serial=True)

    B.barrier("sync", CG)
    roles = [(D, GATEB[None, :]), (2 * D, QKVB[None, D:2 * D]),
             (0, QKVB[None, 0:D]), (3 * D, QKVB[None, 2 * D:3 * D])]
    r_ld = {}

    def brep_issue(ri):
        foff, osrc = roles[ri]
        par = ri % 2
        row = S2[0:1, :] if par == 0 else y1[:].rearrange(
            "p s d -> p (s d)")[0:1, :]
        ch0, ch1 = (B.SW[0], B.SW[1]) if par == 0 else (B.SW[2], B.SW[3])
        if ri - 2 >= 0:
            B.wait("sync", CV, r_sum[ri - 2])
        e1 = B.dma("sync", ch0, row[0:1, 0:1024],
                   BROW.ap()[0:1, foff:foff + D])
        e2 = B.dma("sync", ch1, row[0:1, 1024:2048], osrc)
        r_ld[ri] = (row, ch0, e1, ch1, e2)

    r_sum = {}
    B.barrier("sync", CV, CT)
    brep_issue(0)
    brep_issue(1)
    brow_b = S1[:].bitcast(BF16)
    for ri in range(4):
        row, ch0, e1, ch1, e2 = r_ld[ri]
        bcol = (ri % 2) * 1024
        B.wait("vector", ch0, e1)
        B.wait("vector", ch1, e2)
        B.v(lambda v, row=row: v.tensor_tensor(
            out=row[0:1, 0:1024], in0=row[0:1, 0:1024],
            in1=row[0:1, 1024:2048], op=ALU.add))
        e_bb = B.v(lambda v, row=row, bcol=bcol: v.tensor_copy(
            out=brow_b[0:1, bcol:bcol + 1024], in_=row[0:1, 0:1024]))
        r_sum[ri] = e_bb
        B.wait("tensor", CV, e_bb)
        ps = PB if ri % 2 == 0 else PC
        B.barrier("tensor", CV)
        B.t(lambda t, ps=ps, bcol=bcol: t.matmul(
            out=ps[:, 0:512], lhsT=onerow[:],
            rhs=brow_b[0:1, bcol:bcol + 512], start=True, stop=True))
        e_bm = B.t(lambda t, ps=ps, bcol=bcol: t.matmul(
            out=ps[:, 512:1024], lhsT=onerow[:],
            rhs=brow_b[0:1, bcol + 512:bcol + 1024], start=True,
            stop=True), ev=True)
        B.wait("vector", CT, e_bm)
        B.v(lambda v, ri=ri, ps=ps: v.tensor_copy(
            out=brep[:, ri * 1024:(ri + 1) * 1024], in_=ps[:]), sync=False)
        if ri + 2 < 4:
            brep_issue(ri + 2)

    state = {"x_free": 0, "out_dma": 0, "buf_used": [CT.n, CT.n]}

    def rsqrt_chain(src_col, dst_col, bits=0):
        bi = ibits[:, bits:bits + 2]
        B.v(lambda v: v.tensor_scalar(
            out=bi.bitcast(I32), in0=src_col.bitcast(I32),
            scalar1=0x7FFFFFFF, scalar2=None, op0=ALU.bitwise_xor))
        B.v(lambda v: v.tensor_scalar(
            out=bi.bitcast(I32), in0=bi.bitcast(I32),
            scalar1=1, scalar2=None, op0=ALU.arith_shift_right))
        B.v(lambda v: v.tensor_scalar(
            out=dst_col, in0=bi, scalar1=RSQRT_C, scalar2=None, op0=ALU.mult))
        for _ in range(2):
            B.v(lambda v: v.tensor_tensor(out=bi, in0=dst_col, in1=dst_col,
                                          op=ALU.mult))
            B.v(lambda v: v.scalar_tensor_tensor(
                out=bi, in0=bi, scalar=-0.5, in1=src_col,
                op0=ALU.mult, op1=ALU.mult))
            B.v(lambda v: v.scalar_tensor_tensor(
                out=dst_col, in0=bi, scalar=1.5, in1=dst_col,
                op0=ALU.add, op1=ALU.mult))

    def ln_stats(src, with_ln2, cset=None, bits=0, sum_ready=False):
        if cset is None:
            cset = (C_SUM, C_SQ, C_NMU, C_T1, C_V1,
                    C_S2 if with_ln2 else C_RST)
        SUMc, SQc, NMUc, T1c, V1c, OUTc = cset
        B.barrier("vector", CS)
        if not sum_ready:
            for s_i in range(2):
                B.v(lambda v, s_i=s_i: v.tensor_scalar(
                    out=S1[:, s_i * 1024:(s_i + 1) * 1024],
                    in0=src[:, s_i, :], scalar1=1.0, scalar2=0.0,
                    op0=ALU.mult, op1=ALU.add,
                    accum_out=SUMc[:, s_i:s_i + 1]))
            for s_i in range(2):
                e_sq = B.v(lambda v, s_i=s_i: v.scalar_tensor_tensor(
                    out=S1[:, s_i * 1024:(s_i + 1) * 1024], in0=src[:, s_i, :],
                    scalar=1.0, in1=src[:, s_i, :], op0=ALU.mult,
                    op1=ALU.mult, accum_out=SQc[:, s_i:s_i + 1]), sync=False)
            B.wait("vector", CV, e_sq)
            B.v(lambda v: v.tensor_scalar(out=NMUc, in0=SUMc, scalar1=-INV_D,
                                          scalar2=None, op0=ALU.mult))
            B.v(lambda v: v.tensor_tensor(out=T1c, in0=NMUc, in1=NMUc,
                                          op=ALU.mult))
        else:
            B.wait("scalar", CV, CV.n)
            B.wait("scalar", CT, CT.n)
            e_sqs = B.s(lambda s_: s_.activation(
                out=S1[:, 0:1024], in_=src[:, 0, :], func=AF.Square,
                accum_out=SQc[:, 0:1]), sync=False)
            B.v(lambda v: v.tensor_scalar(out=NMUc, in0=SUMc, scalar1=-INV_D,
                                          scalar2=None, op0=ALU.mult))
            B.v(lambda v: v.tensor_tensor(out=T1c, in0=NMUc, in1=NMUc,
                                          op=ALU.mult))
            e_sq = B.v(lambda v: v.scalar_tensor_tensor(
                out=S1[:, 1024:2048], in0=src[:, 1, :],
                scalar=1.0, in1=src[:, 1, :], op0=ALU.mult, op1=ALU.mult,
                accum_out=SQc[:, 1:2]), sync=False)
            B.wait("vector", CS, e_sqs)
            B.wait("vector", CV, e_sq)
        B.v(lambda v: v.scalar_tensor_tensor(
            out=V1c, in0=SQc, scalar=INV_D, in1=T1c,
            op0=ALU.mult, op1=ALU.subtract))
        if with_ln2:
            B.v(lambda v: v.tensor_scalar(out=V1c, in0=V1c,
                                          scalar1=1.0 + LN_EPS,
                                          scalar2=LN_EPS * LN_EPS,
                                          op0=ALU.mult, op1=ALU.add))
        else:
            B.v(lambda v: v.tensor_scalar(out=V1c, in0=V1c, scalar1=LN_EPS,
                                          scalar2=None, op0=ALU.add))
        rsqrt_chain(V1c, OUTc, bits=bits)

    PF = (scol(32), scol(34), scol(36), scol(38), scol(40), scol(42))
    PF_NMU, PF_RST = PF[2], PF[5]

    def pe_filler(n):
        for _ in range(n):
            B.t(lambda t: t.ldweights(wT[:, 0, 0:128]))

    def ffn_sublayer(src, dst, cidx, prefetched=False, filler=0):
        pe_filler(filler)
        B.barrier("vector", CT)
        if prefetched:
            nmu, rst = PF_NMU, PF_RST
        else:
            ln_stats(src, with_ln2=False, sum_ready=True)
            nmu, rst = C_NMU, C_RST
        e_ap = []
        for s_i in range(2):
            e_ap.append(B.v(
                lambda v, s_i=s_i, nmu=nmu, rst=rst: v.tensor_scalar(
                    out=xst[:, s_i, :], in0=src[:, s_i, :],
                    scalar1=nmu[:, s_i:s_i + 1], scalar2=rst[:, s_i:s_i + 1],
                    op0=ALU.add, op1=ALU.mult), sync=False))
        e_cc = B.v(lambda v, c=carry[cidx]: v.tensor_copy(
            out=xnT[:, :, 0:1].rearrange("p c o -> p (c o)"), in_=c[:]),
            sync=False)
        B.wait("tensor", CS, CS.n)
        B.wait("tensor", CV, e_ap[0])
        for cc in range(8):
            e_t0 = B.t(lambda t, cc=cc: t.transpose(
                out=PT32[:, cc * 128:(cc + 1) * 128],
                in_=xst[:, 0, cc * 128:(cc + 1) * 128], identity=idf[:]),
                ev=(cc == 7))
        B.wait("tensor", CV, e_ap[1])
        for cc in range(8):
            e_t1 = B.t(lambda t, cc=cc: t.transpose(
                out=PA[:, cc * 128:(cc + 1) * 128],
                in_=xst[:, 1, cc * 128:(cc + 1) * 128], identity=idf[:]),
                ev=(cc == 7))
        B.wait("scalar", CT, e_t0)
        B.wait("scalar", CV, e_cc)
        B.barrier("scalar", CV)
        e_e0 = B.s(lambda s_: s_.copy(
            out=xnT[:, :, 1:129],
            in_=PT32[:].rearrange("p (c j) -> p c j", c=8)), sync=False)
        B.wait("vector", CT, e_t1)
        B.v(lambda v: v.tensor_copy(
            out=xnT[:, :, 129:257],
            in_=PA[:].rearrange("p (c j) -> p c j", c=8)))
        B.wait("vector", CS, e_e0)
        B.v(lambda v, c=carry[cidx]: v.tensor_copy(
            out=c[:], in_=xnT[:, :, 256:257].rearrange("p c o -> p (c o)")),
            sync=False)
        B.v(lambda v: v.tensor_sub(
            out=S1[:].rearrange("p (c j) -> p c j", c=8),
            in0=xnT[:, :, 1:257], in1=xnT[:, :, 0:256]))
        e_kmx = {}
        for cc in range(8):
            e_kmx[cc] = B.v(lambda v, cc=cc: v.scalar_tensor_tensor(
                out=xkT[:, cc, :], in0=S1[:, cc * 256:(cc + 1) * 256],
                scalar=murk[:, cc, 1:2], in1=xnT[:, cc, 0:256],
                op0=ALU.mult, op1=ALU.add), sync=False)
        for cc in range(8):
            e_mx = B.v(lambda v, cc=cc: v.scalar_tensor_tensor(
                out=xrT[:, cc, :], in0=S1[:, cc * 256:(cc + 1) * 256],
                scalar=murk[:, cc, 0:1], in1=xnT[:, cc, 0:256],
                op0=ALU.mult, op1=ALU.add), sync=False)
        B.wait("tensor", CS, CS.n)
        e_kb = {}
        for jc in range(8):
            ps = PB if jc < 4 else PC
            col = (jc % 4) * 256
            for cc in range(8):
                if jc == 0:
                    B.wait("tensor", CV, e_kmx[cc])
                e = B.t(lambda t, jc=jc, cc=cc, ps=ps, col=col: t.matmul(
                    out=ps[:, col:col + 256],
                    lhsT=wT[:, cc, D + jc * 128:D + (jc + 1) * 128],
                    rhs=xkT[:, cc, :], start=(cc == 0), stop=(cc == 7)),
                    ev=(cc == 7 and jc % 2 == 1))
            if jc % 2 == 1:
                e_kb[jc // 2] = e
        B.wait("tensor", CV, e_mx)
        e_rm = {}
        for s_i, ps in ((0, PT32), (1, PA)):
            for cc in range(8):
                for h in range(2):
                    e = B.t(lambda t, s_i=s_i, cc=cc, h=h, ps=ps: t.matmul(
                        out=ps[:, h * 512:(h + 1) * 512],
                        lhsT=xrT[:, cc, s_i * 128:(s_i + 1) * 128],
                        rhs=wT[:, cc, h * 512:(h + 1) * 512],
                        start=(cc == 0), stop=(cc == 7)),
                        ev=(cc == 7 and h == 1))
            e_rm[s_i] = e
        for half in range(4):
            ps = PB if half < 2 else PC
            col = (half % 2) * 512
            B.wait("scalar", CT, e_kb[half])
            e_kr = B.s(lambda s_, ps=ps, col=col, half=half: s_.activation(
                out=S1[:, half * 512:(half + 1) * 512],
                in_=ps[:, col:col + 512], func=AF.Relu), sync=False)
        B.wait("scalar", CS, e_kr)
        S1v = S1[:].rearrange("p (c j) -> p c j", c=8)
        e_ka = {}
        for s_i, ps in ((0, PT32), (1, PA)):
            B.wait("scalar", CT, e_rm[s_i])
            e_ka[s_i] = B.s(lambda s_, s_i=s_i: s_.activation(
                out=kaT[:, :, s_i * 128:(s_i + 1) * 128],
                in_=S1v[:, :, s_i * 128:(s_i + 1) * 128],
                func=AF.Square), sync=False)
            e_rv = B.s(lambda s_, s_i=s_i, ps=ps: s_.activation(
                out=xst[:, s_i, :], in_=ps[:], func=AF.Tanh, scale=0.5),
                sync=False)
        B.wait("vector", CS, e_rv)
        e_r5 = B.v(lambda v: v.tensor_scalar(
            out=xst[:].rearrange("p s d -> p (s d)"),
            in0=xst[:].rearrange("p s d -> p (s d)"),
            scalar1=0.5, scalar2=0.5, op0=ALU.mult, op1=ALU.add))
        e_vm = {}
        for s_i, ps in ((0, PB), (1, PC)):
            B.wait("tensor", CS, e_ka[s_i])
            for cc in range(8):
                for h in range(2):
                    e = B.t(lambda t, s_i=s_i, cc=cc, h=h, ps=ps: t.matmul(
                        out=ps[:, h * 512:(h + 1) * 512],
                        lhsT=kaT[:, cc, s_i * 128:(s_i + 1) * 128],
                        rhs=wT[:, cc, 2 * D + h * 512:2 * D + (h + 1) * 512],
                        start=(cc == 0), stop=(cc == 7)),
                        ev=(cc == 7 and h == 1))
            e_vm[s_i] = e
        evs = []
        for s_i, ps in ((0, PB), (1, PC)):
            B.wait("vector", CT, e_vm[s_i])
            B.v(lambda v, s_i=s_i, ps=ps: v.tensor_tensor(
                out=S2[:, s_i * 1024:(s_i + 1) * 1024], in0=xst[:, s_i, :],
                in1=ps[:], op=ALU.mult))
            e_y = B.v(lambda v, s_i=s_i: v.scalar_tensor_tensor(
                out=dst[:, s_i, :], in0=S2[:, s_i * 1024:(s_i + 1) * 1024],
                scalar=1.0, in1=src[:, s_i, :], op0=ALU.mult, op1=ALU.add,
                accum_out=C_SUM[:, s_i:s_i + 1]), sync=False)
            evs.append(e_y)
        return evs

    def load_and_prefetch(seq, ti):
        B.wait("sync", CV, state["x_free"])
        e_x = B.dma("sync", B.SX, xb[:],
                    X[seq, ti * TT:(ti + 1) * TT, :]
                    .rearrange("(s p) d -> p s d", p=128))
        B.wait("scalar", B.SX, e_x)
        B.wait("vector", B.SX, e_x)
        B.barrier("scalar", CT)
        ln_stats(xb, with_ln2=False, cset=PF, bits=2)

    def emit_tile(seq, ti, nxt):
        t0 = ti * TT
        stile = 2 * ti

        evs = ffn_sublayer(xb, y1, 0, prefetched=True, filler=40)
        state["x_free"] = evs[-1]
        B.wait("vector", CV, evs[-1])
        if debug:
            B.wait("sync", CV, evs[-1])
            B.wait("sync", B.SO, B.SO.n)
            ev = B.dma("sync", B.SO,
                       DBG1[seq, t0:t0 + TT, :].rearrange("(s p) d -> p s d", p=128),
                       y1[:])
            B.wait("vector", B.SO, ev)

        pe_filler(130)
        B.barrier("vector", CT)
        ln_stats(y1, with_ln2=True, sum_ready=True)
        e_a2 = []
        for s_i in range(2):
            e_a2.append(B.v(lambda v, s_i=s_i: v.tensor_scalar(
                out=xst[:, s_i, :], in0=y1[:, s_i, :],
                scalar1=C_NMU[:, s_i:s_i + 1], scalar2=C_S2[:, s_i:s_i + 1],
                op0=ALU.add, op1=ALU.mult), sync=False))
        B.wait("tensor", CS, CS.n)
        B.wait("tensor", CV, e_a2[0])
        for cc in range(8):
            e_t2a = B.t(lambda t, cc=cc: t.transpose(
                out=PT32[:, cc * 128:(cc + 1) * 128],
                in_=xst[:, 0, cc * 128:(cc + 1) * 128], identity=idf[:]),
                ev=(cc == 7))
        B.wait("tensor", CV, e_a2[1])
        for cc in range(8):
            e_t2 = B.t(lambda t, cc=cc: t.transpose(
                out=PA[:, cc * 128:(cc + 1) * 128],
                in_=xst[:, 1, cc * 128:(cc + 1) * 128], identity=idf[:]),
                ev=(cc == 7))
        B.wait("scalar", CT, e_t2a)
        e_q = B.s(lambda s_: s_.copy(
            out=xn2T[:, :, 0:128],
            in_=PT32[:].rearrange("p (c j) -> p c j", c=8)), sync=False)
        B.wait("vector", CT, e_t2)
        e_qv = B.v(lambda v: v.tensor_copy(
            out=xn2T[:, :, 128:256],
            in_=PA[:].rearrange("p (c j) -> p c j", c=8)), sync=False)

        def stream_group(grp):
            buf = wstr[grp % 2]
            ch = B.SW[grp % 2]
            B.wait("sync", CT, state["buf_used"][grp % 2])
            ev = B.dma("sync", ch,
                       buf[:].rearrange("p c (a b) -> p c a b", a=4),
                       QS.ap()[grp, :, :, :, :])
            return buf, (ch, ev)

        B.wait("tensor", CS, e_q)
        B.wait("tensor", CV, e_qv)

        S1b = S1[:].bitcast(BF16)
        xnTb = xnT[:].rearrange("p c j -> p (c j)")
        S2v = S2[:].rearrange("p (s d) -> p s d", s=2)
        S1v = S1[:].rearrange("p (s d) -> p s d", s=2)
        xstv = xst
        xstf = xst[:].rearrange("p s d -> p (s d)")
        S2f = S2[:]
        S1f = S1[:]
        bankA = [(PA, 0), (PA, 512), (PB, 0), (PB, 512)]
        bankB = [(PC, 0), (PC, 512), (PT32, 0), (PT32, 512)]
        PCb = PC[:].bitcast(BF16)
        PAb = PA[:].bitcast(BF16)

        def role_mms(groups, bankset):
            bufs = [stream_group(g) for g in groups]
            evs = []
            for gi, (buf, (ch, eld)) in enumerate(bufs):
                B.wait("tensor", ch, eld)
                for tsub in range(2):
                    ps, col = bankset[gi * 2 + tsub]
                    for cc in range(8):
                        e = B.t(lambda t, buf=buf, tsub=tsub, cc=cc, ps=ps,
                                col=col: t.matmul(
                                    out=ps[:, col:col + 512],
                                    lhsT=xn2T[:, cc, tsub * 128:(tsub + 1) * 128],
                                    rhs=buf[:, cc, :],
                                    start=(cc == 0), stop=(cc == 7)),
                                ev=(cc == 7))
                    evs.append(e)
            state["buf_used"][groups[0] % 2] = evs[1]
            state["buf_used"][groups[1] % 2] = evs[3]
            return evs

        def role_evacs(bankset, mm_evs, dst_fn, brep_off):
            evs = []
            for i, (ps, col) in enumerate(bankset):
                gi, tsub = divmod(i, 2)
                B.wait("vector", CT, mm_evs[i])
                e = B.v(lambda v, ps=ps, col=col, gi=gi, tsub=tsub,
                        dst_fn=dst_fn, brep_off=brep_off: v.tensor_tensor(
                            out=dst_fn(gi, tsub),
                            in0=ps[:, col:col + 512],
                            in1=brep[:, brep_off + gi * 512:
                                     brep_off + (gi + 1) * 512],
                            op=ALU.add), sync=False)
                evs.append(e)
            return evs

        eg = role_mms((2, 3), bankA)
        gevs = role_evacs(bankA, eg,
                          lambda gi, tsub: S2v[:, tsub, gi * 512:(gi + 1) * 512],
                          0)
        B.wait("scalar", CV, gevs[-1])
        e_th = B.s(lambda s_: s_.activation(out=S2f, in_=S2f, func=AF.Tanh,
                                            scale=0.5), sync=False)
        ek = role_mms((4, 5), bankB)
        kevs = role_evacs(bankB, ek,
                          lambda gi, tsub: xstv[:, tsub, gi * 512:(gi + 1) * 512],
                          1024)
        B.wait("vector", CS, e_th)
        B.v(lambda v: v.tensor_scalar(out=S2f, in0=S2f, scalar1=0.5,
                                      scalar2=0.5, op0=ALU.mult, op1=ALU.add))
        B.wait("vector", CV, kevs[-1])
        e_kg = B.v(lambda v: v.tensor_tensor(out=xstf, in0=xstf, in1=S2f,
                                             op=ALU.mult))
        B.wait("tensor", CV, gevs[-1])
        eq = role_mms((0, 1), bankA)
        B.barrier("scalar", CV)
        e_ex = B.s(lambda s_: s_.activation(out=S2f, in_=xstf, func=AF.Exp),
                   sync=False)
        B.wait("vector", CS, e_ex)
        B.v(lambda v: v.tensor_scalar(out=S2f, in0=S2f, scalar1=1.0,
                                      scalar2=None, op0=ALU.min))
        B.v(lambda v: v.tensor_scalar(out=xstf, in0=xstf, scalar1=0.0,
                                      scalar2=None, op0=ALU.max))
        e_kp = B.v(lambda v: v.tensor_tensor(out=xnTb[:, 0:2048], in0=S2f,
                                             in1=xstf, op=ALU.add))
        B.wait("tensor", CV, e_kp)
        for sub in range(2):
            for jc in range(8):
                e_ktp = B.t(lambda t, sub=sub, jc=jc: t.transpose(
                    out=PCb[:, sub * 1024 + jc * 128:sub * 1024 + (jc + 1) * 128],
                    in_=xnTb[:, sub * 1024 + jc * 128:
                             sub * 1024 + (jc + 1) * 128].rearrange("p j -> p j"),
                    identity=idb[:]), ev=(jc == 7))
            B.wait("scalar", CT, e_ktp)
            e_kT = B.s(lambda s_, sub=sub, t0=t0: s_.copy(
                out=KT[:, :, t0 + sub * 128:t0 + (sub + 1) * 128],
                in_=PCb[:, sub * 1024:(sub + 1) * 1024]
                .rearrange("p (c j) -> p c j", c=8)), sync=False)
        qevs = role_evacs(bankA, eq,
                          lambda gi, tsub: S1v[:, tsub, gi * 512:(gi + 1) * 512],
                          2048)
        B.barrier("scalar", CV)
        e_ex2 = B.s(lambda s_: s_.activation(out=S2f, in_=S1f, func=AF.Exp),
                    sync=False)
        B.wait("vector", CS, e_ex2)
        B.v(lambda v: v.tensor_scalar(out=S2f, in0=S2f, scalar1=1.0,
                                      scalar2=None, op0=ALU.min))
        B.v(lambda v: v.tensor_scalar(out=S1f, in0=S1f, scalar1=0.0,
                                      scalar2=None, op0=ALU.max))
        e_qp = B.v(lambda v: v.tensor_tensor(out=S1b[:, 0:2048], in0=S2f,
                                             in1=S1f, op=ALU.add))
        B.wait("tensor", CV, e_qp)
        B.wait("tensor", CV, qevs[-1])
        for sub in range(2):
            for jc in range(8):
                e_qtp = B.t(lambda t, sub=sub, jc=jc: t.transpose(
                    out=PAb[:, sub * 1024 + jc * 128:sub * 1024 + (jc + 1) * 128],
                    in_=S1b[:, sub * 1024 + jc * 128:sub * 1024 + (jc + 1) * 128],
                    identity=idb[:]), ev=(jc == 7))
            B.wait("scalar", CT, e_qtp)
            e_qT = B.s(lambda s_, sub=sub: s_.copy(
                out=qT[:, :, sub * 128:(sub + 1) * 128],
                in_=PAb[:, sub * 1024:(sub + 1) * 1024]
                .rearrange("p (c j) -> p c j", c=8)), sync=False)
        B.barrier("tensor", CV, CS)
        ev_ = role_mms((6, 7), bankB)
        role_evacs(bankB, ev_,
                   lambda gi, tsub: Vb[:, stile + tsub, gi * 512:(gi + 1) * 512],
                   3072)

        pe_filler(25)
        B.wait("tensor", CS, e_kT)
        B.wait("tensor", CS, e_qT)
        B.barrier("tensor", CV, CS)
        njs = 2 * ti + 2
        e_qk = {}
        e_aev = {}
        e_av = {}

        def emit_qk(j):
            tlo = 128 if j == (2 * ti + 1) else 0
            pcol = (j % 2) * 512
            if j >= 2:
                ch2, ev2 = e_aev[j - 2]
                B.wait("tensor", CV if ch2 == "v" else CS, ev2)
            for cc in range(8):
                e = B.t(lambda t, j=j, cc=cc, tlo=tlo, pcol=pcol: t.matmul(
                    out=PC[:, pcol:pcol + (TT - tlo)],
                    lhsT=KT[:, cc, j * 128:(j + 1) * 128],
                    rhs=qT[:, cc, tlo:TT],
                    start=(cc == 0), stop=(cc == 7)), ev=(cc == 7))
            e_qk[j] = e
            slot = j % 2
            if j == 2 * ti:
                B.wait("vector", CT, e_qk[j])
                if j >= 2:
                    B.wait("vector", CT, e_av[j - 2])
                B.v(lambda v, pcol=pcol, slot=slot: v.tensor_tensor(
                    out=aT[:, slot, 0:128], in0=PC[:, pcol:pcol + 128],
                    in1=cmask[:], op=ALU.mult), sync=False)
                e_aev[j] = ("v", B.v(lambda v, pcol=pcol, slot=slot:
                                     v.tensor_copy(
                                         out=aT[:, slot, 128:256],
                                         in_=PC[:, pcol + 128:pcol + 256]),
                                     sync=False))
            elif j == 2 * ti + 1:
                B.wait("vector", CT, e_qk[j])
                if j >= 2:
                    B.wait("vector", CT, e_av[j - 2])
                e_aev[j] = ("v", B.v(lambda v, pcol=pcol, slot=slot:
                                     v.tensor_tensor(
                                         out=aT[:, slot, 128:256],
                                         in0=PC[:, pcol:pcol + 128],
                                         in1=cmask[:], op=ALU.mult),
                                     sync=False))
            else:
                B.wait("scalar", CT, e_qk[j])
                if j >= 2:
                    B.wait("scalar", CT, e_av[j - 2])
                e_aev[j] = ("s", B.s(lambda s_, pcol=pcol, slot=slot: s_.copy(
                    out=aT[:, slot, :], in_=PC[:, pcol:pcol + 256]),
                    sync=False))

        def emit_av(j):
            slot = j % 2
            ch1, ev1 = e_aev[j]
            B.wait("tensor", CV if ch1 == "v" else CS, ev1)
            for b in range(2):
                if j > 2 * ti + b:
                    continue
                po = PA if b == 0 else PB
                dencol = b * 512
                first = (j == 0)
                lastj = (j == 2 * ti + b)
                for h in range(2):
                    B.t(lambda t, b=b, h=h, po=po, slot=slot, j=j,
                        first=first, lastj=lastj: t.matmul(
                            out=po[:, h * 512:(h + 1) * 512],
                            lhsT=aT[:, slot, b * 128:(b + 1) * 128],
                            rhs=Vb[:, j, h * 512:(h + 1) * 512],
                            start=first, stop=lastj))
                e_av[j] = B.t(lambda t, b=b, slot=slot, dencol=dencol, j=j,
                              first=first, lastj=lastj: t.matmul(
                                  out=PT32[:, dencol:dencol + 1],
                                  lhsT=aT[:, slot, b * 128:(b + 1) * 128],
                                  rhs=onesb[:], start=first, stop=lastj),
                              ev=True)

        emit_qk(0)
        for j in range(njs):
            if j + 1 < njs:
                emit_qk(j + 1)
            emit_av(j)
            if j == njs // 2 and nxt is not None:
                load_and_prefetch(*nxt)
        if njs // 2 >= njs and nxt is not None:
            load_and_prefetch(*nxt)
        B.wait("vector", CT, e_av[njs - 1])
        B.wait("vector", B.SO, state["out_dma"])
        B.v(lambda v: v.tensor_scalar(out=C_N1[:, 0:1], in0=PT32[:, 0:1],
                                      scalar1=DEN_EPS, scalar2=None,
                                      op0=ALU.add))
        B.v(lambda v: v.tensor_scalar(out=C_N1[:, 1:2], in0=PT32[:, 512:513],
                                      scalar1=DEN_EPS, scalar2=None,
                                      op0=ALU.add))
        rsqrt_chain(C_N1, C_T1)
        B.v(lambda v: v.tensor_tensor(out=C_RD, in0=C_T1, in1=C_T1,
                                      op=ALU.mult))
        e_at = []
        for b in range(2):
            po = PA if b == 0 else PB
            e = B.v(lambda v, b=b, po=po: v.scalar_tensor_tensor(
                out=y2[:, b, :], in0=po[:], scalar=C_RD[:, b:b + 1],
                in1=y1[:, b, :], op0=ALU.mult, op1=ALU.add,
                accum_out=C_SUM[:, b:b + 1]), sync=False)
            e_at.append(e)
        B.wait("vector", CV, e_at[-1])
        if debug and seq == 0 and ti == 0:
            B.barrier("sync", CV, CS, CT)
            for (dst, srcb) in ((DQT, qT), (DKT, KT[:, :, 0:TT]),
                                (DXN, xn2T), (DVB, Vb[:, 0:2, :]),
                                (DAT, aT), (DST, stats)):
                B.wait("sync", B.SO, B.SO.n)
                ev = B.dma("sync", B.SO, dst, srcb[:] if hasattr(srcb, "ap") else srcb)
            B.wait("vector", B.SO, ev)
            B.wait("scalar", B.SO, ev)
            B.wait("tensor", B.SO, ev)
        if debug:
            B.wait("sync", CV, e_at[-1])
            B.wait("sync", B.SO, B.SO.n)
            ev = B.dma("sync", B.SO,
                       DBG2[seq, t0:t0 + TT, :].rearrange("(s p) d -> p s d", p=128),
                       y2[:])
            B.wait("vector", B.SO, ev)

        evs = ffn_sublayer(y2, y2, 1, filler=320)

        B.wait("sync", CV, evs[-1])
        B.wait("sync", B.SO, B.SO.n)
        state["out_dma"] = B.dma(
            "sync", B.SO,
            Y[seq, t0:t0 + TT, :].rearrange("(s p) d -> p s d", p=128),
            y2[:])

    tiles = [(seq, ti) for seq in range(NSEQ) for ti in range(NT)]
    for idx, (seq, ti) in enumerate(tiles):
        if ti == 0:
            B.barrier("vector", CT)
            B.v(lambda v, c=carry[0]: v.memset(c[:], 0.0), sync=False)
            B.v(lambda v, c=carry[1]: v.memset(c[:], 0.0))
        if idx == 0:
            load_and_prefetch(seq, ti)
        nxt = tiles[idx + 1] if idx + 1 < len(tiles) else None
        emit_tile(seq, ti, nxt)

    for c in (CG, B.SC, B.SX, B.SW[0], B.SW[1], B.SO):
        B.wait("sync", c, c.n)

    with nc.Block() as block:
        @block.sync
        def _(sync):
            for fn in B.ops["sync"]:
                fn(sync)

        @block.gpsimd
        def _(g):
            for fn in B.ops["gpsimd"]:
                fn(g)

        @block.vector
        def _(v):
            for fn in B.ops["vector"]:
                fn(v)

        @block.scalar
        def _(s_):
            for fn in B.ops["scalar"]:
                fn(s_)

        @block.tensor
        def _(t):
            for fn in B.ops["tensor"]:
                fn(t)

    return nc


_prog_cache = {}


def kernel(L0_attn_x, L0_attn_qkv_w, L0_attn_qkv_b, L0_attn_gate_w,
           L0_attn_gate_b, L0_attn_ln_g, L0_attn_ln_b, L0_ffn_W_r,
           L0_ffn_W_k, L0_ffn_W_v, L0_ffn_mu_r, L0_ffn_mu_k):
    x = np.ascontiguousarray(np.asarray(L0_attn_x, dtype=np.float32))
    consts = {
        "qkv_w": np.ascontiguousarray(np.asarray(L0_attn_qkv_w, np.float32)),
        "qkv_b": np.ascontiguousarray(np.asarray(L0_attn_qkv_b, np.float32)),
        "gate_w": np.ascontiguousarray(np.asarray(L0_attn_gate_w, np.float32)),
        "gate_b": np.ascontiguousarray(np.asarray(L0_attn_gate_b, np.float32)),
        "ln_g": np.ascontiguousarray(np.asarray(L0_attn_ln_g, np.float32)),
        "ln_b": np.ascontiguousarray(np.asarray(L0_attn_ln_b, np.float32)),
        "W_r": np.ascontiguousarray(np.asarray(L0_ffn_W_r, np.float32)),
        "W_k": np.ascontiguousarray(np.asarray(L0_ffn_W_k, np.float32)),
        "W_v": np.ascontiguousarray(np.asarray(L0_ffn_W_v, np.float32)),
        "mu_r": np.ascontiguousarray(np.asarray(L0_ffn_mu_r, np.float32)),
        "mu_k": np.ascontiguousarray(np.asarray(L0_ffn_mu_k, np.float32)),
        "ident_f": np.eye(128, dtype=np.float32),
        "ident_b": np.eye(128, dtype=np.float32).astype(BFNP),
        "causal_m": np.triu(np.ones((128, 128), np.float32)).astype(BFNP),
    }
    if "nc" not in _prog_cache:
        _prog_cache["nc"] = build_program()
    nc = _prog_cache["nc"]
    in_maps = []
    for c in range(8):
        m = dict(consts)
        m["x"] = np.ascontiguousarray(x[2 * c:2 * c + 2])
        in_maps.append(m)
    res = run_bass_kernel_spmd(nc, in_maps, core_ids=list(range(8)))
    out = np.concatenate([np.asarray(res.results[c]["y"]) for c in range(8)],
                         axis=0)
    return out


# version 33 (final)
# speedup vs baseline: 1.1870x; 1.1870x over previous
import numpy as np
import ml_dtypes
import concourse.bass as bass
import concourse.mybir as mybir
from concourse.bass_utils import run_bass_kernel_spmd

F32 = mybir.dt.float32
BF16 = mybir.dt.bfloat16
I32 = mybir.dt.int32
AF = mybir.ActivationFunctionType
ALU = mybir.AluOpType
AX = mybir.AxisListType
BFNP = ml_dtypes.bfloat16

D = 1024
DC = 8
T = 2048
TT = 256
NT = T // TT
NSEQ = 2
RSQRT_C = 6.144147586977132e18
LN_EPS = 1e-5
DEN_EPS = 1e-6
INV_D = 1.0 / D


class Ch:
    def __init__(self, nc, name):
        self.sem = nc.alloc_semaphore(name)
        self.n = 0


class Builder:

    ENGINES = ("sync", "gpsimd", "vector", "scalar", "tensor")

    def __init__(self, nc):
        self.nc = nc
        self.ops = {e: [] for e in self.ENGINES}
        self.CV = Ch(nc, "cv")
        self.CS = Ch(nc, "cs")
        self.CT = Ch(nc, "ct")
        self.CG = Ch(nc, "cg")
        self.SC = Ch(nc, "sc")
        self.SX = Ch(nc, "sx")
        self.SO = Ch(nc, "so")
        self.SW = [Ch(nc, "sw0"), Ch(nc, "sw1"),
                   Ch(nc, "sw2"), Ch(nc, "sw3")]
        self.CG2 = Ch(nc, "cg2")

    def ch(self, eng):
        return {"vector": self.CV, "scalar": self.CS, "tensor": self.CT,
                "gpsimd": self.CG}[eng]

    def wait(self, eng, ch, val):
        if val > 0:
            self.ops[eng].append(lambda e, s=ch.sem, v=val: e.wait_ge(s, v))

    def barrier(self, eng, *chs):
        for c in chs:
            self.wait(eng, c, c.n)

    def op(self, eng, builder, ev=True):
        ch = self.ch(eng)
        if ev:
            ch.n += 1
            def fn(e, b=builder, sem=ch.sem):
                b(e).then_inc(sem, 1)
            self.ops[eng].append(fn)
            return ch.n
        self.ops[eng].append(lambda e, b=builder: b(e))
        return None

    def dma(self, eng, ch, out, in_, serial=False, **kw):
        ch.n += 16
        def fn(e, o=out, i=in_, k=dict(kw), sem=ch.sem):
            e.dma_start(out=o, in_=i, **k).then_inc(sem, 16)
        self.ops[eng].append(fn)
        if serial:
            self.wait(eng, ch, ch.n)
        return ch.n

    def v(self, builder, sync=True):
        ev = self.op("vector", builder)
        if sync:
            self.wait("vector", self.CV, ev)
        return ev

    def s(self, builder, sync=True):
        ev = self.op("scalar", builder)
        if sync:
            self.wait("scalar", self.CS, ev)
        return ev

    def t(self, builder, ev=False):
        return self.op("tensor", builder, ev=ev)


def build_program(debug=False):
    nc = bass.Bass("TRN2", target_bir_lowering=False, debug=False)

    X = nc.dram_tensor("x", [NSEQ, T, D], F32, kind="ExternalInput").ap()
    Y = nc.dram_tensor("y", [NSEQ, T, D], F32, kind="ExternalOutput").ap()
    QKVW = nc.dram_tensor("qkv_w", [D, 3 * D], F32, kind="ExternalInput").ap()
    QKVB = nc.dram_tensor("qkv_b", [3 * D], F32, kind="ExternalInput").ap()
    GATEW = nc.dram_tensor("gate_w", [D, D], F32, kind="ExternalInput").ap()
    GATEB = nc.dram_tensor("gate_b", [D], F32, kind="ExternalInput").ap()
    LNG = nc.dram_tensor("ln_g", [D], F32, kind="ExternalInput").ap()
    LNB = nc.dram_tensor("ln_b", [D], F32, kind="ExternalInput").ap()
    WR = nc.dram_tensor("W_r", [D, D], F32, kind="ExternalInput").ap()
    WK = nc.dram_tensor("W_k", [D, D], F32, kind="ExternalInput").ap()
    WV = nc.dram_tensor("W_v", [D, D], F32, kind="ExternalInput").ap()
    MUR = nc.dram_tensor("mu_r", [D], F32, kind="ExternalInput").ap()
    MUK = nc.dram_tensor("mu_k", [D], F32, kind="ExternalInput").ap()
    IDF = nc.dram_tensor("ident_f", [128, 128], F32, kind="ExternalInput").ap()
    IDB = nc.dram_tensor("ident_b", [128, 128], BF16, kind="ExternalInput").ap()
    CM = nc.dram_tensor("causal_m", [128, 128], BF16, kind="ExternalInput").ap()

    DBG1 = DBG2 = None
    if debug:
        DBG1 = nc.dram_tensor("dbg_y1", [NSEQ, T, D], F32, kind="ExternalOutput").ap()
        DBG2 = nc.dram_tensor("dbg_y2", [NSEQ, T, D], F32, kind="ExternalOutput").ap()
        DQT = nc.dram_tensor("dbg_qT", [128, DC, TT], BF16, kind="ExternalOutput").ap()
        DKT = nc.dram_tensor("dbg_kT", [128, DC, TT], BF16, kind="ExternalOutput").ap()
        DXN = nc.dram_tensor("dbg_xn2T", [128, DC, TT], BF16, kind="ExternalOutput").ap()
        DVB = nc.dram_tensor("dbg_vb", [128, 2, D], BF16, kind="ExternalOutput").ap()
        DAT = nc.dram_tensor("dbg_aT", [128, 2, TT], BF16, kind="ExternalOutput").ap()
        DST = nc.dram_tensor("dbg_stats", [128, 64], F32, kind="ExternalOutput").ap()
    QS = nc.dram_tensor("qstream", [8, 128, DC, 4, 128], BF16, kind="Internal")
    BROW = nc.dram_tensor("brow", [1, 4 * D], F32, kind="Internal")

    A = nc.alloc_sbuf_tensor
    wT = A("wT", [128, DC, 3 * D], BF16)
    wstr = [A(f"wstr{i}", [128, DC, 512], BF16) for i in range(2)]
    KT = A("KT", [128, DC, T], BF16)
    Vb = A("Vb", [128, T // 128, D], BF16)
    xb = A("xb", [128, 2, D], F32)
    y1 = A("y1", [128, 2, D], F32)
    y2 = A("y2", [128, 2, D], F32)
    xst = A("xst", [128, 2, D], F32)
    xnT = A("xnT", [128, DC, TT + 1], BF16)
    xrT = A("xrT", [128, DC, TT], BF16)
    xkT = A("xkT", [128, DC, TT], BF16)
    kaT = xrT
    qT = xkT
    xn2T = A("xn2T", [128, DC, TT], BF16)
    aT = A("aT", [128, 2, TT], BF16)
    S1 = A("S1", [128, 2048], F32)
    S2 = A("S2", [128, 2048], F32)
    brep = A("brep", [128, 4 * D], BF16)
    murk = A("murk", [128, DC, 2], F32)
    gcol = A("gcol", [128, DC], F32)
    lnbcol = A("lnbcol", [128, DC], F32)
    lnbb = A("lnbb", [128, DC], BF16)
    idf = A("idf", [128, 128], F32)
    idb = A("idb", [128, 128], BF16)
    cmask = A("cmask", [128, 128], BF16)
    onesb = A("onesb", [128, 1], BF16)
    brqb = A("brqb", [1, D], BF16)
    onerow = A("onerow", [1, 128], BF16)
    wstage = A("wstage", [128, 1024], BF16)
    stats = A("stats", [128, 64], F32)
    ibits = A("ibits", [128, 8], F32)
    stag8 = A("stag8", [8, 128], F32)
    carry = [A(f"carry{i}", [128, DC], BF16) for i in range(2)]

    def scol(i, w=2):
        return stats[:, i:i + w]
    C_SUM, C_SQ, C_NMU, C_VAR, C_V1, C_RST = (
        scol(0), scol(2), scol(4), scol(6), scol(8), scol(10))
    C_T1, C_V2, C_S2, C_RD, C_N1 = scol(12), scol(14), scol(16), scol(18), scol(20)

    PA = nc.alloc_psum_tensor("PA", [128, 1024], F32)
    PB = nc.alloc_psum_tensor("PB", [128, 1024], F32)
    PC = nc.alloc_psum_tensor("PC", [128, 1024], F32)
    PT = nc.alloc_psum_tensor("PT", [128, 2048], BF16)
    PT32 = PT[:].bitcast(F32)

    B = Builder(nc)
    CV, CS, CT, CG = B.CV, B.CS, B.CT, B.CG

    B.dma("sync", B.SC, idf[:], IDF, serial=True)
    B.dma("sync", B.SC, idb[:], IDB, serial=True)
    e_cst = B.dma("sync", B.SC, cmask[:], CM, serial=True)
    B.wait("vector", B.SC, e_cst)
    B.v(lambda v: v.memset(onesb[:], 1.0), sync=False)
    B.v(lambda v: v.memset(onerow[:], 1.0), sync=False)

    def redist(src_1d, dst_col):
        B.barrier("sync", CS)
        e_ld = B.dma("sync", B.SC, stag8[:],
                     src_1d.rearrange("(c p) -> c p", p=128), serial=True)
        B.wait("tensor", B.SC, e_ld)
        B.barrier("tensor", CS, CV)
        e_tp = B.t(lambda t: t.transpose(out=PC[:, 0:8], in_=stag8[:],
                                         identity=idf[0:8, 0:8]), ev=True)
        B.wait("scalar", CT, e_tp)
        return B.s(lambda s_, d=dst_col: s_.copy(out=d, in_=PC[:, 0:8]),
                   sync=False)

    redist(MUR, murk[:, :, 0])
    redist(MUK, murk[:, :, 1])
    redist(LNG, gcol[:])
    e_lnb = redist(LNB, lnbcol[:])
    B.wait("vector", CS, e_lnb)
    B.v(lambda v: v.tensor_copy(out=lnbb[:], in_=lnbcol[:]), sync=False)

    witems = []
    for (WSRC, off, scale) in [(WR, 0, 1.0), (WK, D, 1.0), (WV, 2 * D, 0.5)]:
        for dp in range(4):
            witems.append((WSRC, off, scale, dp))
    wld = {}
    cast_done = {}

    def w_issue(it):
        WSRC, off, scale, dp = witems[it]
        par = it % 2
        sbuf = S1 if par == 0 else S2
        if it - 2 in cast_done:
            B.wait("sync", CV, cast_done[it - 2])
        wld[it] = B.dma("sync", B.SW[par],
                        sbuf[:].rearrange("p (a b) -> p a b", a=2),
                        WSRC[dp * 256:(dp + 1) * 256, :]
                        .rearrange("(a p) d -> p a d", p=128))

    w_issue(0)
    w_issue(1)
    wstage2 = xrT[:].rearrange("p c j -> p (c j)")
    tp_done = {}
    ev_done = {}
    for it, (WSRC, off, scale, dp) in enumerate(witems):
        par = it % 2
        sbuf = S1 if par == 0 else S2
        for half in range(2):
            k = it * 2 + half
            dd = dp * 2 + half
            stg = wstage[:] if k % 2 == 0 else wstage2[:, 0:1024]
            pth = (k % 2) * 1024
            if k - 2 in tp_done:
                B.wait("vector", CT, tp_done[k - 2])
            B.wait("vector", B.SW[par], wld[it])
            e_c = B.v(lambda v, sc=scale, sbuf=sbuf, half=half, stg=stg:
                      v.tensor_scalar(
                          out=stg, in0=sbuf[:, half * 1024:(half + 1) * 1024],
                          scalar1=sc, scalar2=None, op0=ALU.mult), sync=False)
            cast_done[it] = e_c
            B.wait("tensor", CV, e_c)
            if k - 2 in ev_done:
                B.wait("tensor", CS, ev_done[k - 2])
            for cc in range(8):
                e_tp = B.t(lambda t, cc=cc, stg=stg, pth=pth: t.transpose(
                    out=PT[:, pth + cc * 128:pth + (cc + 1) * 128],
                    in_=stg[:, cc * 128:(cc + 1) * 128], identity=idb[:]),
                    ev=(cc == 7))
            tp_done[k] = e_tp
            B.wait("scalar", CT, e_tp)
            ev_done[k] = B.s(lambda s_, dd=dd, off=off, pth=pth: s_.copy(
                out=wT[:, :, off + dd * 128: off + (dd + 1) * 128],
                in_=PT[:, pth:pth + 1024].rearrange("p (c j) -> p c j", c=8)),
                sync=False)
        if it + 2 < len(witems):
            w_issue(it + 2)

    def slab_of(part, jg):
        if part == 0:
            if jg < 2:
                return jg * 4
            if jg < 4:
                return 16 + (jg - 2) * 4
            return 24 + (jg - 4) * 4
        return 8 + jg * 4

    fold_regions = [(PA, 0), (PA, 512), (PB, 0), (PB, 512),
                    (PC, 0), (PC, 512), (PT32, 0), (PT32, 512)]
    B.barrier("tensor", CS, CV)
    fitems = []
    for part, (WSRC, jtot) in enumerate([(QKVW, 3 * D), (GATEW, D)]):
        for jg in range(jtot // 512):
            fitems.append((part, WSRC, jg))
    Vbig = Vb[:].rearrange("p a b -> p (a b)")
    y1f = y1[:].rearrange("p s d -> p (s d)")
    xstf0 = xst[:].rearrange("p s d -> p (s d)")
    f_ld = {}
    f_cast = {}
    f_mm = {}

    def fold_issue(it):
        part, WSRC, jg = fitems[it]
        par = it % 2
        d0, d1 = (S1[:], S2[:]) if par == 0 else (xstf0, y1f)
        ch0, ch1 = (B.SW[0], B.SW[1]) if par == 0 else (B.SW[2], B.SW[3])
        if it - 2 in f_cast:
            B.wait("sync", CV, f_cast[it - 2])
            B.wait("sync", CT, f_mm[it - 2])
        if par == 0:
            v0 = S1[:].rearrange("p (a b) -> p a b", a=4)
            v1 = S2[:].rearrange("p (a b) -> p a b", a=4)
        else:
            v0 = xst[:].rearrange("p s (a b) -> p (s a) b", a=2)
            v1 = y1[:].rearrange("p s (a b) -> p (s a) b", a=2)
        e0 = B.dma("sync", ch0, v0,
                   WSRC[0:512, jg * 512:(jg + 1) * 512]
                   .rearrange("(a p) d -> p a d", p=128))
        e1 = B.dma("sync", ch1, v1,
                   WSRC[512:1024, jg * 512:(jg + 1) * 512]
                   .rearrange("(a p) d -> p a d", p=128))
        f_ld[it] = (ch0, e0, ch1, e1)

    B.barrier("sync", CV, CT)
    fold_issue(0)
    fold_issue(1)
    for it, (part, WSRC, jg) in enumerate(fitems):
        grp = slab_of(part, jg) // 4
        ps, pcol = fold_regions[grp]
        par = it % 2
        d0, d1 = (S1[:], S2[:]) if par == 0 else (xstf0, y1f)
        ch0, e0, ch1, e1 = f_ld[it]
        B.wait("tensor", ch0, e0)
        B.wait("tensor", ch1, e1)
        for cc in range(8):
            sbuf = d0 if cc < 4 else d1
            e_bm = B.t(lambda t, cc=cc, ps=ps, pcol=pcol, sbuf=sbuf: t.matmul(
                out=ps[0:1, pcol:pcol + 512], lhsT=lnbcol[:, cc:cc + 1],
                rhs=sbuf[:, (cc % 4) * 512:(cc % 4 + 1) * 512],
                start=(cc == 0), stop=(cc == 7)), ev=(cc == 7))
        f_mm[it] = e_bm
        B.wait("vector", ch0, e0)
        B.wait("vector", ch1, e1)
        vcol = (it % 2) * 4096
        if it - 2 in f_cast:
            pass
        B.wait("vector", B.CG if it % 2 == 0 else B.CG2,
               (B.CG if it % 2 == 0 else B.CG2).n)
        for cc in range(8):
            sbuf = d0 if cc < 4 else d1
            e_c = B.v(lambda v, cc=cc, sbuf=sbuf, vcol=vcol: v.tensor_scalar(
                out=Vbig[:, vcol + cc * 512:vcol + (cc + 1) * 512],
                in0=sbuf[:, (cc % 4) * 512:(cc % 4 + 1) * 512],
                scalar1=gcol[:, cc:cc + 1], scalar2=None, op0=ALU.mult),
                sync=False)
        f_cast[it] = e_c
        gch = B.CG if it % 2 == 0 else B.CG2
        B.wait("gpsimd", CV, e_c)
        B.dma("gpsimd", gch, QS.ap()[grp, :, :, :, :],
              Vbig[:, vcol:vcol + 4096].rearrange("p (c a b) -> p c a b",
                                                  c=8, a=4))
        if it + 2 < len(fitems):
            fold_issue(it + 2)
    B.wait("sync", B.CG2, B.CG2.n)

    B.barrier("vector", CT)
    for grp in range(8):
        ps, pcol = fold_regions[grp]
        B.barrier("vector", CG)
        e_ev = B.v(lambda v, ps=ps, pcol=pcol: v.tensor_copy(
            out=S2[0:1, 0:512], in_=ps[0:1, pcol:pcol + 512]), sync=False)
        B.wait("gpsimd", CV, e_ev)
        B.dma("gpsimd", CG, BROW.ap()[0:1, grp * 512:(grp + 1) * 512],
              S2[0:1, 0:512], serial=True)

    B.barrier("sync", CG)
    roles = [(D, GATEB[None, :]), (2 * D, QKVB[None, D:2 * D]),
             (0, QKVB[None, 0:D]), (3 * D, QKVB[None, 2 * D:3 * D])]
    r_ld = {}

    def brep_issue(ri):
        foff, osrc = roles[ri]
        par = ri % 2
        row = S2[0:1, :] if par == 0 else y1[:].rearrange(
            "p s d -> p (s d)")[0:1, :]
        ch0, ch1 = (B.SW[0], B.SW[1]) if par == 0 else (B.SW[2], B.SW[3])
        if ri - 2 >= 0:
            B.wait("sync", CV, r_sum[ri - 2])
        e1 = B.dma("sync", ch0, row[0:1, 0:1024],
                   BROW.ap()[0:1, foff:foff + D])
        e2 = B.dma("sync", ch1, row[0:1, 1024:2048], osrc)
        r_ld[ri] = (row, ch0, e1, ch1, e2)

    r_sum = {}
    B.barrier("sync", CV, CT)
    brep_issue(0)
    brep_issue(1)
    brow_b = S1[:].bitcast(BF16)
    for ri in range(4):
        row, ch0, e1, ch1, e2 = r_ld[ri]
        bcol = (ri % 2) * 1024
        B.wait("vector", ch0, e1)
        B.wait("vector", ch1, e2)
        B.v(lambda v, row=row: v.tensor_tensor(
            out=row[0:1, 0:1024], in0=row[0:1, 0:1024],
            in1=row[0:1, 1024:2048], op=ALU.add))
        e_bb = B.v(lambda v, row=row, bcol=bcol: v.tensor_copy(
            out=brow_b[0:1, bcol:bcol + 1024], in_=row[0:1, 0:1024]))
        r_sum[ri] = e_bb
        B.wait("tensor", CV, e_bb)
        ps = PB if ri % 2 == 0 else PC
        B.barrier("tensor", CV)
        B.t(lambda t, ps=ps, bcol=bcol: t.matmul(
            out=ps[:, 0:512], lhsT=onerow[:],
            rhs=brow_b[0:1, bcol:bcol + 512], start=True, stop=True))
        e_bm = B.t(lambda t, ps=ps, bcol=bcol: t.matmul(
            out=ps[:, 512:1024], lhsT=onerow[:],
            rhs=brow_b[0:1, bcol + 512:bcol + 1024], start=True,
            stop=True), ev=True)
        B.wait("vector", CT, e_bm)
        B.v(lambda v, ri=ri, ps=ps: v.tensor_copy(
            out=brep[:, ri * 1024:(ri + 1) * 1024], in_=ps[:]), sync=False)
        if ri + 2 < 4:
            brep_issue(ri + 2)

    state = {"x_free": 0, "out_dma": 0, "buf_used": [CT.n, CT.n]}

    def rsqrt_chain(src_col, dst_col, bits=0):
        bi = ibits[:, bits:bits + 2]
        B.v(lambda v: v.tensor_scalar(
            out=bi.bitcast(I32), in0=src_col.bitcast(I32),
            scalar1=0x7FFFFFFF, scalar2=None, op0=ALU.bitwise_xor))
        B.v(lambda v: v.tensor_scalar(
            out=bi.bitcast(I32), in0=bi.bitcast(I32),
            scalar1=1, scalar2=None, op0=ALU.arith_shift_right))
        B.v(lambda v: v.tensor_scalar(
            out=dst_col, in0=bi, scalar1=RSQRT_C, scalar2=None, op0=ALU.mult))
        for _ in range(2):
            B.v(lambda v: v.tensor_tensor(out=bi, in0=dst_col, in1=dst_col,
                                          op=ALU.mult))
            B.v(lambda v: v.scalar_tensor_tensor(
                out=bi, in0=bi, scalar=-0.5, in1=src_col,
                op0=ALU.mult, op1=ALU.mult))
            B.v(lambda v: v.scalar_tensor_tensor(
                out=dst_col, in0=bi, scalar=1.5, in1=dst_col,
                op0=ALU.add, op1=ALU.mult))

    def ln_stats(src, with_ln2, cset=None, bits=0, sum_ready=False):
        if cset is None:
            cset = (C_SUM, C_SQ, C_NMU, C_T1, C_V1,
                    C_S2 if with_ln2 else C_RST)
        SUMc, SQc, NMUc, T1c, V1c, OUTc = cset
        B.barrier("vector", CS)
        if not sum_ready:
            for s_i in range(2):
                B.v(lambda v, s_i=s_i: v.tensor_scalar(
                    out=S1[:, s_i * 1024:(s_i + 1) * 1024],
                    in0=src[:, s_i, :], scalar1=1.0, scalar2=0.0,
                    op0=ALU.mult, op1=ALU.add,
                    accum_out=SUMc[:, s_i:s_i + 1]))
            for s_i in range(2):
                e_sq = B.v(lambda v, s_i=s_i: v.scalar_tensor_tensor(
                    out=S1[:, s_i * 1024:(s_i + 1) * 1024], in0=src[:, s_i, :],
                    scalar=1.0, in1=src[:, s_i, :], op0=ALU.mult,
                    op1=ALU.mult, accum_out=SQc[:, s_i:s_i + 1]), sync=False)
            B.wait("vector", CV, e_sq)
            B.v(lambda v: v.tensor_scalar(out=NMUc, in0=SUMc, scalar1=-INV_D,
                                          scalar2=None, op0=ALU.mult))
            B.v(lambda v: v.tensor_tensor(out=T1c, in0=NMUc, in1=NMUc,
                                          op=ALU.mult))
        else:
            B.wait("scalar", CV, CV.n)
            B.wait("scalar", CT, CT.n)
            e_sqs = B.s(lambda s_: s_.activation(
                out=S1[:, 0:1024], in_=src[:, 0, :], func=AF.Square,
                accum_out=SQc[:, 0:1]), sync=False)
            B.v(lambda v: v.tensor_scalar(out=NMUc, in0=SUMc, scalar1=-INV_D,
                                          scalar2=None, op0=ALU.mult))
            B.v(lambda v: v.tensor_tensor(out=T1c, in0=NMUc, in1=NMUc,
                                          op=ALU.mult))
            e_sq = B.v(lambda v: v.scalar_tensor_tensor(
                out=S1[:, 1024:2048], in0=src[:, 1, :],
                scalar=1.0, in1=src[:, 1, :], op0=ALU.mult, op1=ALU.mult,
                accum_out=SQc[:, 1:2]), sync=False)
            B.wait("vector", CS, e_sqs)
            B.wait("vector", CV, e_sq)
        B.v(lambda v: v.scalar_tensor_tensor(
            out=V1c, in0=SQc, scalar=INV_D, in1=T1c,
            op0=ALU.mult, op1=ALU.subtract))
        if with_ln2:
            B.v(lambda v: v.tensor_scalar(out=V1c, in0=V1c,
                                          scalar1=1.0 + LN_EPS,
                                          scalar2=LN_EPS * LN_EPS,
                                          op0=ALU.mult, op1=ALU.add))
        else:
            B.v(lambda v: v.tensor_scalar(out=V1c, in0=V1c, scalar1=LN_EPS,
                                          scalar2=None, op0=ALU.add))
        rsqrt_chain(V1c, OUTc, bits=bits)

    PF = (scol(32), scol(34), scol(36), scol(38), scol(40), scol(42))
    PF_NMU, PF_RST = PF[2], PF[5]

    def pe_filler(n):
        for _ in range(n):
            B.t(lambda t: t.ldweights(wT[:, 0, 0:128]))

    def ffn_sublayer(src, dst, cidx, prefetched=False, filler=0):
        pe_filler(filler)
        B.barrier("vector", CT)
        if prefetched:
            nmu, rst = PF_NMU, PF_RST
        else:
            ln_stats(src, with_ln2=False, sum_ready=True)
            nmu, rst = C_NMU, C_RST
        e_ap = []
        for s_i in range(2):
            e_ap.append(B.v(
                lambda v, s_i=s_i, nmu=nmu, rst=rst: v.tensor_scalar(
                    out=xst[:, s_i, :], in0=src[:, s_i, :],
                    scalar1=nmu[:, s_i:s_i + 1], scalar2=rst[:, s_i:s_i + 1],
                    op0=ALU.add, op1=ALU.mult), sync=False))
        e_cc = B.v(lambda v, c=carry[cidx]: v.tensor_copy(
            out=xnT[:, :, 0:1].rearrange("p c o -> p (c o)"), in_=c[:]),
            sync=False)
        B.wait("tensor", CS, CS.n)
        B.wait("tensor", CV, e_ap[0])
        for cc in range(8):
            e_t0 = B.t(lambda t, cc=cc: t.transpose(
                out=PT32[:, cc * 128:(cc + 1) * 128],
                in_=xst[:, 0, cc * 128:(cc + 1) * 128], identity=idf[:]),
                ev=(cc == 7))
        B.wait("tensor", CV, e_ap[1])
        for cc in range(8):
            e_t1 = B.t(lambda t, cc=cc: t.transpose(
                out=PA[:, cc * 128:(cc + 1) * 128],
                in_=xst[:, 1, cc * 128:(cc + 1) * 128], identity=idf[:]),
                ev=(cc == 7))
        B.wait("scalar", CT, e_t0)
        B.wait("scalar", CV, e_cc)
        B.barrier("scalar", CV)
        e_e0 = B.s(lambda s_: s_.copy(
            out=xnT[:, :, 1:129],
            in_=PT32[:].rearrange("p (c j) -> p c j", c=8)), sync=False)
        B.wait("vector", CT, e_t1)
        B.v(lambda v: v.tensor_copy(
            out=xnT[:, :, 129:257],
            in_=PA[:].rearrange("p (c j) -> p c j", c=8)))
        B.wait("vector", CS, e_e0)
        B.v(lambda v, c=carry[cidx]: v.tensor_copy(
            out=c[:], in_=xnT[:, :, 256:257].rearrange("p c o -> p (c o)")),
            sync=False)
        B.v(lambda v: v.tensor_sub(
            out=S1[:].rearrange("p (c j) -> p c j", c=8),
            in0=xnT[:, :, 1:257], in1=xnT[:, :, 0:256]))
        e_kmx = {}
        for cc in range(8):
            e_kmx[cc] = B.v(lambda v, cc=cc: v.scalar_tensor_tensor(
                out=xkT[:, cc, :], in0=S1[:, cc * 256:(cc + 1) * 256],
                scalar=murk[:, cc, 1:2], in1=xnT[:, cc, 0:256],
                op0=ALU.mult, op1=ALU.add), sync=False)
        e_rmx = {}
        for cc in range(8):
            e_rmx[cc] = B.v(lambda v, cc=cc: v.scalar_tensor_tensor(
                out=xrT[:, cc, :], in0=S1[:, cc * 256:(cc + 1) * 256],
                scalar=murk[:, cc, 0:1], in1=xnT[:, cc, 0:256],
                op0=ALU.mult, op1=ALU.add), sync=False)
        B.wait("tensor", CS, CS.n)
        e_kb = {}
        for jc in range(8):
            ps = PB if jc < 4 else PC
            col = (jc % 4) * 256
            for cc in range(8):
                if jc == 0:
                    B.wait("tensor", CV, e_kmx[cc])
                e = B.t(lambda t, jc=jc, cc=cc, ps=ps, col=col: t.matmul(
                    out=ps[:, col:col + 256],
                    lhsT=wT[:, cc, D + jc * 128:D + (jc + 1) * 128],
                    rhs=xkT[:, cc, :], start=(cc == 0), stop=(cc == 7)),
                    ev=(cc == 7 and jc % 2 == 1))
            if jc % 2 == 1:
                e_kb[jc // 2] = e
        B.wait("tensor", CV, e_rmx[7])
        e_rm = {}
        for s_i, ps in ((0, PT32), (1, PA)):
            for cc in range(8):
                for h in range(2):
                    e = B.t(lambda t, s_i=s_i, cc=cc, h=h, ps=ps: t.matmul(
                        out=ps[:, h * 512:(h + 1) * 512],
                        lhsT=xrT[:, cc, s_i * 128:(s_i + 1) * 128],
                        rhs=wT[:, cc, h * 512:(h + 1) * 512],
                        start=(cc == 0), stop=(cc == 7)),
                        ev=(cc == 7 and h == 1))
            e_rm[s_i] = e
        for half in range(4):
            ps = PB if half < 2 else PC
            col = (half % 2) * 512
            B.wait("scalar", CV, e_rmx[2 * half + 1])
            B.wait("scalar", CT, e_kb[half])
            e_kr = B.s(lambda s_, ps=ps, col=col, half=half: s_.activation(
                out=S1[:, half * 512:(half + 1) * 512],
                in_=ps[:, col:col + 512], func=AF.Relu), sync=False)
        B.wait("scalar", CS, e_kr)
        S1v = S1[:].rearrange("p (c j) -> p c j", c=8)
        e_ka = {}
        for s_i, ps in ((0, PT32), (1, PA)):
            B.wait("scalar", CT, e_rm[s_i])
            e_ka[s_i] = B.s(lambda s_, s_i=s_i: s_.activation(
                out=kaT[:, :, s_i * 128:(s_i + 1) * 128],
                in_=S1v[:, :, s_i * 128:(s_i + 1) * 128],
                func=AF.Square), sync=False)
            e_rv = B.s(lambda s_, s_i=s_i, ps=ps: s_.activation(
                out=xst[:, s_i, :], in_=ps[:], func=AF.Tanh, scale=0.5),
                sync=False)
        B.wait("vector", CS, e_rv)
        e_r5 = B.v(lambda v: v.tensor_scalar(
            out=xst[:].rearrange("p s d -> p (s d)"),
            in0=xst[:].rearrange("p s d -> p (s d)"),
            scalar1=0.5, scalar2=0.5, op0=ALU.mult, op1=ALU.add))
        e_vm = {}
        for s_i, ps in ((0, PB), (1, PC)):
            B.wait("tensor", CS, e_ka[s_i])
            for cc in range(8):
                for h in range(2):
                    e = B.t(lambda t, s_i=s_i, cc=cc, h=h, ps=ps: t.matmul(
                        out=ps[:, h * 512:(h + 1) * 512],
                        lhsT=kaT[:, cc, s_i * 128:(s_i + 1) * 128],
                        rhs=wT[:, cc, 2 * D + h * 512:2 * D + (h + 1) * 512],
                        start=(cc == 0), stop=(cc == 7)),
                        ev=(cc == 7 and h == 1))
            e_vm[s_i] = e
        evs = []
        for s_i, ps in ((0, PB), (1, PC)):
            B.wait("vector", CT, e_vm[s_i])
            B.v(lambda v, s_i=s_i, ps=ps: v.tensor_tensor(
                out=S2[:, s_i * 1024:(s_i + 1) * 1024], in0=xst[:, s_i, :],
                in1=ps[:], op=ALU.mult))
            e_y = B.v(lambda v, s_i=s_i: v.scalar_tensor_tensor(
                out=dst[:, s_i, :], in0=S2[:, s_i * 1024:(s_i + 1) * 1024],
                scalar=1.0, in1=src[:, s_i, :], op0=ALU.mult, op1=ALU.add,
                accum_out=C_SUM[:, s_i:s_i + 1]), sync=False)
            evs.append(e_y)
        return evs

    def load_and_prefetch(seq, ti):
        B.wait("sync", CV, state["x_free"])
        e_x = B.dma("sync", B.SX, xb[:],
                    X[seq, ti * TT:(ti + 1) * TT, :]
                    .rearrange("(s p) d -> p s d", p=128))
        B.wait("scalar", B.SX, e_x)
        B.wait("vector", B.SX, e_x)
        B.barrier("scalar", CT)
        ln_stats(xb, with_ln2=False, cset=PF, bits=2)

    def emit_tile(seq, ti, nxt):
        t0 = ti * TT
        stile = 2 * ti

        evs = ffn_sublayer(xb, y1, 0, prefetched=True, filler=40)
        state["x_free"] = evs[-1]
        B.wait("vector", CV, evs[-1])
        if debug:
            B.wait("sync", CV, evs[-1])
            B.wait("sync", B.SO, B.SO.n)
            ev = B.dma("sync", B.SO,
                       DBG1[seq, t0:t0 + TT, :].rearrange("(s p) d -> p s d", p=128),
                       y1[:])
            B.wait("vector", B.SO, ev)

        pe_filler(130)
        B.barrier("vector", CT)
        ln_stats(y1, with_ln2=True, sum_ready=True)
        e_a2 = []
        for s_i in range(2):
            e_a2.append(B.v(lambda v, s_i=s_i: v.tensor_scalar(
                out=xst[:, s_i, :], in0=y1[:, s_i, :],
                scalar1=C_NMU[:, s_i:s_i + 1], scalar2=C_S2[:, s_i:s_i + 1],
                op0=ALU.add, op1=ALU.mult), sync=False))
        B.wait("tensor", CS, CS.n)
        B.wait("tensor", CV, e_a2[0])
        for cc in range(8):
            e_t2a = B.t(lambda t, cc=cc: t.transpose(
                out=PT32[:, cc * 128:(cc + 1) * 128],
                in_=xst[:, 0, cc * 128:(cc + 1) * 128], identity=idf[:]),
                ev=(cc == 7))
        B.wait("tensor", CV, e_a2[1])
        for cc in range(8):
            e_t2 = B.t(lambda t, cc=cc: t.transpose(
                out=PA[:, cc * 128:(cc + 1) * 128],
                in_=xst[:, 1, cc * 128:(cc + 1) * 128], identity=idf[:]),
                ev=(cc == 7))
        B.wait("scalar", CT, e_t2a)
        e_q = B.s(lambda s_: s_.copy(
            out=xn2T[:, :, 0:128],
            in_=PT32[:].rearrange("p (c j) -> p c j", c=8)), sync=False)
        B.wait("vector", CT, e_t2)
        e_qv = B.v(lambda v: v.tensor_copy(
            out=xn2T[:, :, 128:256],
            in_=PA[:].rearrange("p (c j) -> p c j", c=8)), sync=False)

        def stream_group(grp):
            buf = wstr[grp % 2]
            ch = B.SW[grp % 2]
            B.wait("sync", CT, state["buf_used"][grp % 2])
            ev = B.dma("sync", ch,
                       buf[:].rearrange("p c (a b) -> p c a b", a=4),
                       QS.ap()[grp, :, :, :, :])
            return buf, (ch, ev)

        B.wait("tensor", CS, e_q)
        B.wait("tensor", CV, e_qv)

        S1b = S1[:].bitcast(BF16)
        xnTb = xnT[:].rearrange("p c j -> p (c j)")
        S2v = S2[:].rearrange("p (s d) -> p s d", s=2)
        S1v = S1[:].rearrange("p (s d) -> p s d", s=2)
        xstv = xst
        xstf = xst[:].rearrange("p s d -> p (s d)")
        S2f = S2[:]
        S1f = S1[:]
        bankA = [(PA, 0), (PA, 512), (PB, 0), (PB, 512)]
        bankB = [(PC, 0), (PC, 512), (PT32, 0), (PT32, 512)]
        PCb = PC[:].bitcast(BF16)
        PAb = PA[:].bitcast(BF16)

        def role_mms(groups, bankset):
            bufs = [stream_group(g) for g in groups]
            evs = []
            for gi, (buf, (ch, eld)) in enumerate(bufs):
                B.wait("tensor", ch, eld)
                for tsub in range(2):
                    ps, col = bankset[gi * 2 + tsub]
                    for cc in range(8):
                        e = B.t(lambda t, buf=buf, tsub=tsub, cc=cc, ps=ps,
                                col=col: t.matmul(
                                    out=ps[:, col:col + 512],
                                    lhsT=xn2T[:, cc, tsub * 128:(tsub + 1) * 128],
                                    rhs=buf[:, cc, :],
                                    start=(cc == 0), stop=(cc == 7)),
                                ev=(cc == 7))
                    evs.append(e)
            state["buf_used"][groups[0] % 2] = evs[1]
            state["buf_used"][groups[1] % 2] = evs[3]
            return evs

        def role_evacs(bankset, mm_evs, dst_fn, brep_off):
            evs = []
            for i, (ps, col) in enumerate(bankset):
                gi, tsub = divmod(i, 2)
                B.wait("vector", CT, mm_evs[i])
                e = B.v(lambda v, ps=ps, col=col, gi=gi, tsub=tsub,
                        dst_fn=dst_fn, brep_off=brep_off: v.tensor_tensor(
                            out=dst_fn(gi, tsub),
                            in0=ps[:, col:col + 512],
                            in1=brep[:, brep_off + gi * 512:
                                     brep_off + (gi + 1) * 512],
                            op=ALU.add), sync=False)
                evs.append(e)
            return evs

        eg = role_mms((2, 3), bankA)
        gevs = role_evacs(bankA, eg,
                          lambda gi, tsub: S2v[:, tsub, gi * 512:(gi + 1) * 512],
                          0)
        B.wait("scalar", CV, gevs[-1])
        e_th = B.s(lambda s_: s_.activation(out=S2f, in_=S2f, func=AF.Tanh,
                                            scale=0.5), sync=False)
        ek = role_mms((4, 5), bankB)
        kevs = role_evacs(bankB, ek,
                          lambda gi, tsub: xstv[:, tsub, gi * 512:(gi + 1) * 512],
                          1024)
        B.wait("vector", CS, e_th)
        B.v(lambda v: v.tensor_scalar(out=S2f, in0=S2f, scalar1=0.5,
                                      scalar2=0.5, op0=ALU.mult, op1=ALU.add))
        B.wait("vector", CV, kevs[-1])
        e_kg = B.v(lambda v: v.tensor_tensor(out=xstf, in0=xstf, in1=S2f,
                                             op=ALU.mult))
        B.wait("tensor", CV, gevs[-1])
        eq = role_mms((0, 1), bankA)
        B.barrier("scalar", CV)
        e_ex = B.s(lambda s_: s_.activation(out=S2f, in_=xstf, func=AF.Exp),
                   sync=False)
        B.wait("vector", CS, e_ex)
        B.v(lambda v: v.tensor_scalar(out=S2f, in0=S2f, scalar1=1.0,
                                      scalar2=None, op0=ALU.min))
        B.v(lambda v: v.tensor_scalar(out=xstf, in0=xstf, scalar1=0.0,
                                      scalar2=None, op0=ALU.max))
        e_kp = B.v(lambda v: v.tensor_tensor(out=xnTb[:, 0:2048], in0=S2f,
                                             in1=xstf, op=ALU.add))
        B.wait("tensor", CV, e_kp)
        for sub in range(2):
            for jc in range(8):
                e_ktp = B.t(lambda t, sub=sub, jc=jc: t.transpose(
                    out=PCb[:, sub * 1024 + jc * 128:sub * 1024 + (jc + 1) * 128],
                    in_=xnTb[:, sub * 1024 + jc * 128:
                             sub * 1024 + (jc + 1) * 128].rearrange("p j -> p j"),
                    identity=idb[:]), ev=(jc == 7))
            B.wait("scalar", CT, e_ktp)
            e_kT = B.s(lambda s_, sub=sub, t0=t0: s_.copy(
                out=KT[:, :, t0 + sub * 128:t0 + (sub + 1) * 128],
                in_=PCb[:, sub * 1024:(sub + 1) * 1024]
                .rearrange("p (c j) -> p c j", c=8)), sync=False)
        qevs = role_evacs(bankA, eq,
                          lambda gi, tsub: S1v[:, tsub, gi * 512:(gi + 1) * 512],
                          2048)
        B.barrier("scalar", CV)
        e_ex2 = B.s(lambda s_: s_.activation(out=S2f, in_=S1f, func=AF.Exp),
                    sync=False)
        B.wait("vector", CS, e_ex2)
        B.v(lambda v: v.tensor_scalar(out=S2f, in0=S2f, scalar1=1.0,
                                      scalar2=None, op0=ALU.min))
        B.v(lambda v: v.tensor_scalar(out=S1f, in0=S1f, scalar1=0.0,
                                      scalar2=None, op0=ALU.max))
        e_qp = B.v(lambda v: v.tensor_tensor(out=S1b[:, 0:2048], in0=S2f,
                                             in1=S1f, op=ALU.add))
        B.wait("tensor", CV, e_qp)
        B.wait("tensor", CV, qevs[-1])
        for sub in range(2):
            for jc in range(8):
                e_qtp = B.t(lambda t, sub=sub, jc=jc: t.transpose(
                    out=PAb[:, sub * 1024 + jc * 128:sub * 1024 + (jc + 1) * 128],
                    in_=S1b[:, sub * 1024 + jc * 128:sub * 1024 + (jc + 1) * 128],
                    identity=idb[:]), ev=(jc == 7))
            B.wait("scalar", CT, e_qtp)
            e_qT = B.s(lambda s_, sub=sub: s_.copy(
                out=qT[:, :, sub * 128:(sub + 1) * 128],
                in_=PAb[:, sub * 1024:(sub + 1) * 1024]
                .rearrange("p (c j) -> p c j", c=8)), sync=False)
        B.barrier("tensor", CV, CS)
        ev_ = role_mms((6, 7), bankB)
        role_evacs(bankB, ev_,
                   lambda gi, tsub: Vb[:, stile + tsub, gi * 512:(gi + 1) * 512],
                   3072)

        pe_filler(25)
        B.wait("tensor", CS, e_kT)
        B.wait("tensor", CS, e_qT)
        B.barrier("tensor", CV, CS)
        njs = 2 * ti + 2
        e_qk = {}
        e_aev = {}
        e_av = {}

        def emit_qk(j):
            tlo = 128 if j == (2 * ti + 1) else 0
            pcol = (j % 2) * 512
            if j >= 2:
                ch2, ev2 = e_aev[j - 2]
                B.wait("tensor", CV if ch2 == "v" else CS, ev2)
            for cc in range(8):
                e = B.t(lambda t, j=j, cc=cc, tlo=tlo, pcol=pcol: t.matmul(
                    out=PC[:, pcol:pcol + (TT - tlo)],
                    lhsT=KT[:, cc, j * 128:(j + 1) * 128],
                    rhs=qT[:, cc, tlo:TT],
                    start=(cc == 0), stop=(cc == 7)), ev=(cc == 7))
            e_qk[j] = e
            slot = j % 2
            if j == 2 * ti:
                B.wait("vector", CT, e_qk[j])
                if j >= 2:
                    B.wait("vector", CT, e_av[j - 2])
                B.v(lambda v, pcol=pcol, slot=slot: v.tensor_tensor(
                    out=aT[:, slot, 0:128], in0=PC[:, pcol:pcol + 128],
                    in1=cmask[:], op=ALU.mult), sync=False)
                e_aev[j] = ("v", B.v(lambda v, pcol=pcol, slot=slot:
                                     v.tensor_copy(
                                         out=aT[:, slot, 128:256],
                                         in_=PC[:, pcol + 128:pcol + 256]),
                                     sync=False))
            elif j == 2 * ti + 1:
                B.wait("vector", CT, e_qk[j])
                if j >= 2:
                    B.wait("vector", CT, e_av[j - 2])
                e_aev[j] = ("v", B.v(lambda v, pcol=pcol, slot=slot:
                                     v.tensor_tensor(
                                         out=aT[:, slot, 128:256],
                                         in0=PC[:, pcol:pcol + 128],
                                         in1=cmask[:], op=ALU.mult),
                                     sync=False))
            else:
                B.wait("scalar", CT, e_qk[j])
                if j >= 2:
                    B.wait("scalar", CT, e_av[j - 2])
                e_aev[j] = ("s", B.s(lambda s_, pcol=pcol, slot=slot: s_.copy(
                    out=aT[:, slot, :], in_=PC[:, pcol:pcol + 256]),
                    sync=False))

        def emit_av(j):
            slot = j % 2
            ch1, ev1 = e_aev[j]
            B.wait("tensor", CV if ch1 == "v" else CS, ev1)
            for b in range(2):
                if j > 2 * ti + b:
                    continue
                po = PA if b == 0 else PB
                dencol = b * 512
                first = (j == 0)
                lastj = (j == 2 * ti + b)
                for h in range(2):
                    B.t(lambda t, b=b, h=h, po=po, slot=slot, j=j,
                        first=first, lastj=lastj: t.matmul(
                            out=po[:, h * 512:(h + 1) * 512],
                            lhsT=aT[:, slot, b * 128:(b + 1) * 128],
                            rhs=Vb[:, j, h * 512:(h + 1) * 512],
                            start=first, stop=lastj))
                e_av[j] = B.t(lambda t, b=b, slot=slot, dencol=dencol, j=j,
                              first=first, lastj=lastj: t.matmul(
                                  out=PT32[:, dencol:dencol + 1],
                                  lhsT=aT[:, slot, b * 128:(b + 1) * 128],
                                  rhs=onesb[:], start=first, stop=lastj),
                              ev=True)

        emit_qk(0)
        for j in range(njs):
            if j + 1 < njs:
                emit_qk(j + 1)
            emit_av(j)
            if j == njs // 2 and nxt is not None:
                load_and_prefetch(*nxt)
        if njs // 2 >= njs and nxt is not None:
            load_and_prefetch(*nxt)
        B.wait("vector", CT, e_av[njs - 1])
        B.wait("vector", B.SO, state["out_dma"])
        B.v(lambda v: v.tensor_scalar(out=C_N1[:, 0:1], in0=PT32[:, 0:1],
                                      scalar1=DEN_EPS, scalar2=None,
                                      op0=ALU.add))
        B.v(lambda v: v.tensor_scalar(out=C_N1[:, 1:2], in0=PT32[:, 512:513],
                                      scalar1=DEN_EPS, scalar2=None,
                                      op0=ALU.add))
        rsqrt_chain(C_N1, C_T1)
        B.v(lambda v: v.tensor_tensor(out=C_RD, in0=C_T1, in1=C_T1,
                                      op=ALU.mult))
        e_at = []
        for b in range(2):
            po = PA if b == 0 else PB
            e = B.v(lambda v, b=b, po=po: v.scalar_tensor_tensor(
                out=y2[:, b, :], in0=po[:], scalar=C_RD[:, b:b + 1],
                in1=y1[:, b, :], op0=ALU.mult, op1=ALU.add,
                accum_out=C_SUM[:, b:b + 1]), sync=False)
            e_at.append(e)
        B.wait("vector", CV, e_at[-1])
        if debug and seq == 0 and ti == 0:
            B.barrier("sync", CV, CS, CT)
            for (dst, srcb) in ((DQT, qT), (DKT, KT[:, :, 0:TT]),
                                (DXN, xn2T), (DVB, Vb[:, 0:2, :]),
                                (DAT, aT), (DST, stats)):
                B.wait("sync", B.SO, B.SO.n)
                ev = B.dma("sync", B.SO, dst, srcb[:] if hasattr(srcb, "ap") else srcb)
            B.wait("vector", B.SO, ev)
            B.wait("scalar", B.SO, ev)
            B.wait("tensor", B.SO, ev)
        if debug:
            B.wait("sync", CV, e_at[-1])
            B.wait("sync", B.SO, B.SO.n)
            ev = B.dma("sync", B.SO,
                       DBG2[seq, t0:t0 + TT, :].rearrange("(s p) d -> p s d", p=128),
                       y2[:])
            B.wait("vector", B.SO, ev)

        evs = ffn_sublayer(y2, y2, 1, filler=320)

        B.wait("sync", CV, evs[-1])
        B.wait("sync", B.SO, B.SO.n)
        state["out_dma"] = B.dma(
            "sync", B.SO,
            Y[seq, t0:t0 + TT, :].rearrange("(s p) d -> p s d", p=128),
            y2[:])

    tiles = [(seq, ti) for seq in range(NSEQ) for ti in range(NT)]
    for idx, (seq, ti) in enumerate(tiles):
        if ti == 0:
            B.barrier("vector", CT)
            B.v(lambda v, c=carry[0]: v.memset(c[:], 0.0), sync=False)
            B.v(lambda v, c=carry[1]: v.memset(c[:], 0.0))
        if idx == 0:
            load_and_prefetch(seq, ti)
        nxt = tiles[idx + 1] if idx + 1 < len(tiles) else None
        emit_tile(seq, ti, nxt)

    for c in (CG, B.SC, B.SX, B.SW[0], B.SW[1], B.SO):
        B.wait("sync", c, c.n)

    with nc.Block() as block:
        @block.sync
        def _(sync):
            for fn in B.ops["sync"]:
                fn(sync)

        @block.gpsimd
        def _(g):
            for fn in B.ops["gpsimd"]:
                fn(g)

        @block.vector
        def _(v):
            for fn in B.ops["vector"]:
                fn(v)

        @block.scalar
        def _(s_):
            for fn in B.ops["scalar"]:
                fn(s_)

        @block.tensor
        def _(t):
            for fn in B.ops["tensor"]:
                fn(t)

    return nc


_prog_cache = {}


def kernel(L0_attn_x, L0_attn_qkv_w, L0_attn_qkv_b, L0_attn_gate_w,
           L0_attn_gate_b, L0_attn_ln_g, L0_attn_ln_b, L0_ffn_W_r,
           L0_ffn_W_k, L0_ffn_W_v, L0_ffn_mu_r, L0_ffn_mu_k):
    x = np.ascontiguousarray(np.asarray(L0_attn_x, dtype=np.float32))
    consts = {
        "qkv_w": np.ascontiguousarray(np.asarray(L0_attn_qkv_w, np.float32)),
        "qkv_b": np.ascontiguousarray(np.asarray(L0_attn_qkv_b, np.float32)),
        "gate_w": np.ascontiguousarray(np.asarray(L0_attn_gate_w, np.float32)),
        "gate_b": np.ascontiguousarray(np.asarray(L0_attn_gate_b, np.float32)),
        "ln_g": np.ascontiguousarray(np.asarray(L0_attn_ln_g, np.float32)),
        "ln_b": np.ascontiguousarray(np.asarray(L0_attn_ln_b, np.float32)),
        "W_r": np.ascontiguousarray(np.asarray(L0_ffn_W_r, np.float32)),
        "W_k": np.ascontiguousarray(np.asarray(L0_ffn_W_k, np.float32)),
        "W_v": np.ascontiguousarray(np.asarray(L0_ffn_W_v, np.float32)),
        "mu_r": np.ascontiguousarray(np.asarray(L0_ffn_mu_r, np.float32)),
        "mu_k": np.ascontiguousarray(np.asarray(L0_ffn_mu_k, np.float32)),
        "ident_f": np.eye(128, dtype=np.float32),
        "ident_b": np.eye(128, dtype=np.float32).astype(BFNP),
        "causal_m": np.triu(np.ones((128, 128), np.float32)).astype(BFNP),
    }
    if "nc" not in _prog_cache:
        _prog_cache["nc"] = build_program()
    nc = _prog_cache["nc"]
    in_maps = []
    for c in range(8):
        m = dict(consts)
        m["x"] = np.ascontiguousarray(x[2 * c:2 * c + 2])
        in_maps.append(m)
    res = run_bass_kernel_spmd(nc, in_maps, core_ids=list(range(8)))
    out = np.concatenate([np.asarray(res.results[c]["y"]) for c in range(8)],
                         axis=0)
    return out
